# version 01
# baseline (speedup 1.0000x reference)
import numpy as np
import concourse.bass as bass
import concourse.mybir as mybir
from concourse.bass_utils import run_bass_kernel_spmd

F32 = mybir.dt.float32
AF = mybir.ActivationFunctionType
ALU = mybir.AluOpType
AX = mybir.AxisListType

NC = 8
B = 4
T = 4096
D = 1024
ROWS = B * T
RPC = ROWS // NC
EPS = 1e-5


def f32(a):
    return np.ascontiguousarray(np.asarray(a), dtype=np.float32)


class Prog:

    def __init__(self):
        self.nc = bass.Bass("TRN2", target_bir_lowering=False)
        self.items = []
        self.consts = {}

    def din(self, name, shape):
        return self.nc.dram_tensor(name, list(shape), F32, kind="ExternalInput").ap()

    def dout(self, name, shape):
        return self.nc.dram_tensor(name, list(shape), F32, kind="ExternalOutput").ap()

    def sb(self, name, p, f):
        return self.nc.alloc_sbuf_tensor(name, [p, f], F32)

    def ps(self, name, p, f):
        return self.nc.alloc_psum_tensor(name, [p, f], F32)

    def _e(self, eng, meth, *args, inc=1, **kw):
        self.items.append((eng, meth, args, kw, inc))

    def dma(self, out, in_, **kw):
        self._e("sync", "dma_start", inc=16, out=out, in_=in_, **kw)

    def mm(self, out, lhsT, rhs, first, last):
        self._e("tensor", "matmul", out, lhsT, rhs, start=first, stop=last,
                skip_group_check=True)

    def tr(self, out_psum, in_, ident):
        if not isinstance(ident, bass.AP):
            ident = ident[:, :]
        self._e("tensor", "transpose", out_psum, in_, ident)

    def act(self, out, in_, func, bias=0.0, scale=1.0):
        for v in (bias, scale):
            if isinstance(v, float) and v not in (0.0, 1.0):
                self.consts.setdefault(v, len(self.consts))
        self._e("scalar", "activation", out, in_, func, bias=bias, scale=scale)

    def scopy(self, out, in_):
        self._e("scalar", "copy", out, in_)

    def smul(self, out, in_, c):
        self._e("scalar", "mul", out, in_, c)

    def vtt(self, out, a, b, op):
        self._e("vector", "tensor_tensor", out, a, b, op=op)

    def vts(self, out, in_, scalar, op):
        self._e("vector", "tensor_single_scalar", out, in_, scalar, op)

    def vstt(self, out, in0, scalar, in1, op0, op1):
        self._e("vector", "scalar_tensor_tensor", out, in0, scalar, in1, op0, op1)

    def vred(self, out, in_, op):
        self._e("vector", "tensor_reduce", out, in_, AX.X, op)

    def vrecip(self, out, in_):
        self._e("vector", "reciprocal", out, in_)

    def vmemset(self, ap, c):
        self._e("vector", "memset", ap, c)

    def vcopy(self, out, in_):
        self._e("vector", "tensor_copy", out, in_)

    def ln(self, dst, src, W, scr, c1, c2, P=128):
        self.vred(c1[0:P, 0:1], src, ALU.add)
        self.smul(c2[0:P, 0:1], c1[0:P, 0:1], 1.0 / W)
        self.vts(scr, src, c2[0:P, 0:1], ALU.subtract)
        self.act(dst, scr, AF.Square)
        self.vred(c1[0:P, 0:1], dst, ALU.add)
        self.act(c2[0:P, 0:1], c1[0:P, 0:1], AF.Sqrt, bias=EPS, scale=1.0 / W)
        self.vrecip(c1[0:P, 0:1], c2[0:P, 0:1])
        self.vts(dst, scr, c1[0:P, 0:1], ALU.mult)

    def bcast(self, dst, row, W, ones_r, pscr):
        for o in range(0, W, 512):
            w = min(512, W - o)
            self.mm(pscr[:, 0:w], ones_r[0:1, 0:128], row[0:1, o:o + w], True, True)
            self.scopy(dst[:, o:o + w], pscr[:, 0:w])

    def transp(self, dst, src, W, ident, pscr):
        for o in range(0, W, 128):
            w = min(128, W - o)
            self.tr(pscr[0:w, 0:128], src[:, o:o + w], ident)
            self.scopy(dst[0:w, o:o + 128], pscr[0:w, 0:128])

    def build(self):
        nc = self.nc
        if self.consts:
            ctile = nc.alloc_sbuf_tensor("cconsts", [128, len(self.consts)], F32)
            pre = []
            for v, i in self.consts.items():
                nc.const_aps.register(F32, v, ctile[:, i:i + 1])
                pre.append(("vector", "memset", (ctile[:, i:i + 1], v), {}, 1))
            self.items = pre + self.items
        total = sum(it[4] for it in self.items)
        used = sorted(set(it[0] for it in self.items))
        with nc.semaphore() as S, nc.Block() as blk:
            def mk(name):
                def body(eng):
                    cum = 0
                    for (en, meth, args, kw, inc) in self.items:
                        if en == name:
                            if cum > 0:
                                eng.wait_ge(S, cum)
                            getattr(eng, meth)(*args, **kw).then_inc(S, inc)
                        cum += inc
                    eng.wait_ge(S, total)
                return body
            for name in used:
                getattr(blk, name)(mk(name))
        return nc


_cache = {}


def run(key, buildfn, in_maps):
    if key not in _cache:
        _cache[key] = buildfn()
    nc = _cache[key]
    res = run_bass_kernel_spmd(nc, in_maps, core_ids=list(range(NC)))
    return res.results


def build_lnproj(widths, pre_ln=True):
    p = Prog()
    x = p.din("x", (RPC, D))
    ws = [p.din(f"w{i}", (D, w)) for i, w in enumerate(widths)]
    outs = [p.dout(f"o{i}", (RPC, w)) for i, w in enumerate(widths)]
    ident = p.din("ident", (128, 128))

    w_sb = [p.sb(f"w{i}sb", 128, 8 * w) for i, w in enumerate(widths)]
    id_sb = p.sb("idsb", 128, 128)
    xt = p.sb("xt", 128, D)
    scr = p.sb("scr", 128, D)
    h = p.sb("h", 128, D)
    hT = p.sb("hT", 128, D)
    c1 = p.sb("c1", 128, 1)
    c2 = p.sb("c2", 128, 1)
    pT = p.ps("pT", 128, 128)
    pM = p.ps("pM", 128, 512)

    p.dma(id_sb[:, :], ident)
    for i, w in enumerate(widths):
        for kt in range(8):
            p.dma(w_sb[i][:, kt * w:(kt + 1) * w], ws[i][kt * 128:(kt + 1) * 128, :])

    for t in range(16):
        p.dma(xt[:, :], x[t * 128:(t + 1) * 128, :])
        if pre_ln:
            p.ln(h[:, :], xt[:, :], D, scr[:, :], c1, c2)
        else:
            p.scopy(h[:, :], xt[:, :])
        p.transp(hT, h, D, id_sb, pT)
        for i, w in enumerate(widths):
            for o in range(0, w, 512):
                n = min(512, w - o)
                for kt in range(8):
                    p.mm(pM[:, 0:n], hT[:, kt * 128:(kt + 1) * 128],
                         w_sb[i][:, kt * w + o: kt * w + o + n], kt == 0, kt == 7)
                p.scopy(scr[:, 0:n], pM[:, 0:n])
                p.dma(outs[i][t * 128:(t + 1) * 128, o:o + n], scr[:, 0:n])
    return p.build()


def launch_lnproj(xfull, weights, pre_ln=True, tag=""):
    widths = [w.shape[1] for w in weights]
    key = ("lnproj", tuple(widths), pre_ln)
    ident = f32(np.eye(128))
    in_maps = []
    for c in range(NC):
        m = {"x": f32(xfull[c * RPC:(c + 1) * RPC]), "ident": ident}
        for i, w in enumerate(weights):
            m[f"w{i}"] = f32(w)
        in_maps.append(m)
    res = run(key, lambda: build_lnproj(widths, pre_ln), in_maps)
    return [np.concatenate([res[c][f"o{i}"] for c in range(NC)], 0)
            for i in range(len(widths))]


def build_projres(K, out_z):
    p = Prog()
    a = p.din("a", (RPC, K))
    w = p.din("w", (K, D))
    x = p.din("x", (RPC, D))
    xo = p.dout("xo", (RPC, D))
    z = p.dout("z", (RPC, D)) if out_z else None
    ident = p.din("ident", (128, 128))

    nk = K // 128 if K >= 128 else 1
    w_sb = p.sb("wsb", 128 if K >= 128 else K, nk * D if K >= 128 else D)
    id_sb = p.sb("idsb", 128, 128)
    at = p.sb("at", 128, K)
    aT = p.sb("aT", 128 if K >= 128 else K, 128 * (nk if K >= 128 else 1))
    y = p.sb("y", 128, D)
    scr = p.sb("scr", 128, D)
    xt = p.sb("xt", 128, D)
    c1 = p.sb("c1", 128, 1)
    c2 = p.sb("c2", 128, 1)
    pT = p.ps("pT", 128, 128)
    pM = p.ps("pM", 128, 512)

    p.dma(id_sb[:, :], ident)
    if K >= 128:
        for kt in range(nk):
            p.dma(w_sb[:, kt * D:(kt + 1) * D], w[kt * 128:(kt + 1) * 128, :])
    else:
        p.dma(w_sb[:, :], w)

    for t in range(16):
        p.dma(at[:, 0:K], a[t * 128:(t + 1) * 128, :])
        if K >= 128:
            for kt in range(nk):
                p.tr(pT[:, :], at[:, kt * 128:(kt + 1) * 128], id_sb)
                p.scopy(aT[:, kt * 128:(kt + 1) * 128], pT[:, :])
        else:
            p.tr(pT[0:K, 0:128], at[:, 0:K], id_sb)
            p.scopy(aT[0:K, 0:128], pT[0:K, 0:128])
        for o in range(0, D, 512):
            if K >= 128:
                for kt in range(nk):
                    p.mm(pM[:, :], aT[:, kt * 128:(kt + 1) * 128],
                         w_sb[:, kt * D + o: kt * D + o + 512], kt == 0, kt == nk - 1)
            else:
                p.mm(pM[:, :], aT[0:K, 0:128], w_sb[0:K, o:o + 512], True, True)
            p.scopy(y[:, o:o + 512], pM[:, :])
        p.ln(y[:, :], y[:, :], D, scr[:, :], c1, c2)
        p.dma(xt[:, :], x[t * 128:(t + 1) * 128, :])
        p.vtt(xt[:, :], xt[:, :], y[:, :], ALU.add)
        p.dma(xo[t * 128:(t + 1) * 128, :], xt[:, :])
        if out_z:
            p.ln(y[:, :], xt[:, :], D, scr[:, :], c1, c2)
            p.dma(z[t * 128:(t + 1) * 128, :], y[:, :])
    return p.build()


def launch_projres(afull, w, xfull, out_z, tag=""):
    K = afull.shape[1]
    key = ("projres", K, out_z)
    ident = f32(np.eye(128))
    in_maps = [{"a": f32(afull[c * RPC:(c + 1) * RPC]), "w": f32(w),
                "x": f32(xfull[c * RPC:(c + 1) * RPC]), "ident": ident}
               for c in range(NC)]
    res = run(key, lambda: build_projres(K, out_z), in_maps)
    xo = np.concatenate([res[c]["xo"] for c in range(NC)], 0)
    if out_z:
        zf = np.concatenate([res[c]["z"] for c in range(NC)], 0)
        return xo, zf
    return xo


def build_l0attn():
    p = Prog()
    QT = p.din("QT", (8, 8, 64, 512))
    KT = p.din("KT", (8, 8, 64, 512))
    Q = p.din("Q", (8, 8, 512, 64))
    K_ = p.din("K", (8, 8, 512, 64))
    V = p.din("V", (8, 8, 512, 64))
    BET = p.din("BET", (8, 64))
    ones_r = p.din("ones_r", (1, 128))
    ones_c = p.din("ones_c", (128, 1))
    ATT = p.dout("ATT", (8, 8, 512, 64))

    on_r = p.sb("on_r", 1, 128)
    on_c = p.sb("on_c", 128, 1)
    qt_s = p.sb("qt_s", 64, 512)
    kt_s = p.sb("kt_s", 64, 512)
    q_sb = p.sb("q_sb", 128, 256)
    k_sb = p.sb("k_sb", 128, 256)
    vo = p.sb("vo", 128, 4 * 65)
    ska = p.sb("ska", 128, 256)
    sqa = p.sb("sqa", 128, 256)
    sqT = p.sb("sqT", 64, 512)
    tA = p.sb("tA", 128, 512)
    tB = p.sb("tB", 128, 512)
    et = p.sb("et", 128, 2048)
    rs = p.sb("rs", 128, 4)
    irs = p.sb("irs", 128, 4)
    mem = p.sb("mem", 64, 64)
    beta = p.sb("beta", 1, 64)
    gate = p.sb("gate", 1, 64)
    r0r = p.sb("r0r", 1, 64)
    r1r = p.sb("r1r", 1, 64)
    rskr = p.sb("rskr", 1, 64)
    rb = p.sb("rb", 128, 128)
    invd = p.sb("invd", 128, 1)
    t1 = p.sb("t1", 128, 64)
    t2 = p.sb("t2", 128, 64)
    att = p.sb("att", 128, 64)

    pM = p.ps("pM", 64, 64)
    pZ = p.ps("pZ", 1, 64)
    pS = p.ps("pS", 128, 512)
    pA0 = p.ps("pA0", 128, 130)
    pA1 = p.ps("pA1", 128, 130)
    pB = p.ps("pB", 128, 128)
    pN = p.ps("pN", 128, 64)

    p.dma(on_r[:, :], ones_r)
    p.dma(on_c[:, :], ones_c)
    for st in range(4):
        p.vmemset(vo[:, st * 65 + 64: st * 65 + 65], 1.0)

    for pr in range(8):
        p.dma(beta[:, :], BET[pr:pr + 1, :])
        p.act(gate[:, :], beta[:, :], AF.Sigmoid)
        p.vtt(r0r[:, :], on_r[0:1, 0:64], gate[:, :], ALU.subtract)
        p.vmemset(mem[:, :], 0.0)
        for sg in range(8):
            p.dma(qt_s[:, :], QT[pr, sg])
            p.dma(kt_s[:, :], KT[pr, sg])
            for st in range(4):
                p.dma(q_sb[:, st * 64:(st + 1) * 64], Q[pr, sg, st * 128:(st + 1) * 128, :])
                p.dma(k_sb[:, st * 64:(st + 1) * 64], K_[pr, sg, st * 128:(st + 1) * 128, :])
                p.dma(vo[:, st * 65: st * 65 + 64], V[pr, sg, st * 128:(st + 1) * 128, :])
            if sg > 0:
                p.scopy(mem[:, :], pM[:, :])
            p.vts(tA[:, 0:256], k_sb[:, :], 0.0, ALU.min)
            p.act(tB[:, 0:256], tA[:, 0:256], AF.Exp)
            p.vts(tA[:, 0:256], k_sb[:, :], 0.0, ALU.max)
            p.vtt(ska[:, :], tA[:, 0:256], tB[:, 0:256], ALU.add)
            for st in range(4):
                p.mm(pM[:, :], ska[:, st * 64:(st + 1) * 64],
                     vo[:, st * 65: st * 65 + 64], sg == 0 and st == 0, st == 3)
                p.mm(pZ[:, :], on_c[:, 0:1], ska[:, st * 64:(st + 1) * 64],
                     sg == 0 and st == 0, st == 3)
            for sjt in range(4):
                p.mm(pS[:, :], kt_s[:, sjt * 128:(sjt + 1) * 128], qt_s[:, :], True, True)
                p.act(et[:, sjt * 512:(sjt + 1) * 512], pS[:, :], AF.Exp, scale=0.125)
            p.vts(tA[0:64, 0:512], qt_s[:, :], 0.0, ALU.min)
            p.act(tB[0:64, 0:512], tA[0:64, 0:512], AF.Exp)
            p.vts(tA[0:64, 0:512], qt_s[:, :], 0.0, ALU.max)
            p.vtt(sqT[:, :], tA[0:64, 0:512], tB[0:64, 0:512], ALU.add)
            p.vts(tA[:, 0:256], q_sb[:, :], 0.0, ALU.min)
            p.act(tB[:, 0:256], tA[:, 0:256], AF.Exp)
            p.vts(tA[:, 0:256], q_sb[:, :], 0.0, ALU.max)
            p.vtt(sqa[:, :], tA[:, 0:256], tB[:, 0:256], ALU.add)
            for st in range(4):
                p.vred(rs[:, st:st + 1], sqa[:, st * 64:(st + 1) * 64], ALU.add)
            p.vrecip(irs[:, :], rs[:, :])
            p.vrecip(rskr[:, :], pZ[0:1, :])
            p.vtt(r1r[:, :], gate[:, :], rskr[:, :], ALU.mult)
            p.mm(pB[:, 0:64], on_r[0:1, :], r1r[0:1, :], True, True)
            p.mm(pB[:, 64:128], on_r[0:1, :], r0r[0:1, :], True, True)
            p.scopy(rb[:, :], pB[:, :])
            for sit in range(4):
                pa = pA0 if sit < 2 else pA1
                sl = (sit % 2) * 65
                for sjt in range(4):
                    p.mm(pa[:, sl:sl + 65],
                         et[:, sjt * 512 + sit * 128: sjt * 512 + (sit + 1) * 128],
                         vo[:, sjt * 65:(sjt + 1) * 65], sjt == 0, sjt == 3)
                p.mm(pN[:, :], sqT[:, sit * 128:(sit + 1) * 128], mem[:, :], True, True)
                p.vrecip(invd[:, :], pa[:, sl + 64:sl + 65])
                p.vts(t2[:, :], pa[:, sl:sl + 64], invd[:, 0:1], ALU.mult)
                p.vtt(t2[:, :], t2[:, :], rb[:, 64:128], ALU.mult)
                p.vts(t1[:, :], pN[:, :], irs[:, sit:sit + 1], ALU.mult)
                p.vtt(t1[:, :], t1[:, :], rb[:, 0:64], ALU.mult)
                p.vtt(att[:, :], t1[:, :], t2[:, :], ALU.add)
                p.dma(ATT[pr, sg, sit * 128:(sit + 1) * 128, :], att[:, :])
    return p.build()


def launch_l0attn(qf, kf, vf, betas):
    key = ("l0attn",)
    def bands(m):
        return np.ascontiguousarray(
            m.reshape(B, 8, 16, 32, D).transpose(0, 2, 1, 3, 4).reshape(64, 8, 512, 64))
    Qa, Ka, Va = bands(qf), bands(kf), bands(vf)
    QTa = np.ascontiguousarray(Qa.transpose(0, 1, 3, 2))
    KTa = np.ascontiguousarray(Ka.transpose(0, 1, 3, 2))
    bet = np.asarray(betas).reshape(16, 64)
    betp = np.repeat(bet[None, :, :], B, 0).reshape(64, 64)
    ones_r = f32(np.ones((1, 128)))
    ones_c = f32(np.ones((128, 1)))
    in_maps = []
    for c in range(NC):
        sl = slice(c * 8, (c + 1) * 8)
        in_maps.append({"QT": f32(QTa[sl]), "KT": f32(KTa[sl]), "Q": f32(Qa[sl]),
                        "K": f32(Ka[sl]), "V": f32(Va[sl]), "BET": f32(betp[sl]),
                        "ones_r": ones_r, "ones_c": ones_c})
    res = run(key, build_l0attn, in_maps)
    att = np.concatenate([res[c]["ATT"] for c in range(NC)], 0)
    att_nat = att.reshape(B, 16, 8, 32, 16, 64).transpose(0, 2, 1, 3, 4, 5).reshape(ROWS, D)
    return np.ascontiguousarray(att_nat)


def build_l0ffn():
    p = Prog()
    xe = p.din("xe", (RPC + 1, D))
    wr = p.din("wr", (D, D))
    wk = p.din("wk", (D, D))
    wv = p.din("wv", (D, D))
    mur = p.din("mur", (1, D))
    muk = p.din("muk", (1, D))
    ident = p.din("ident", (128, 128))
    ones_r = p.din("ones_r", (1, 128))
    x2 = p.dout("x2", (RPC, D))

    id_sb = p.sb("idsb", 128, 128)
    on_r = p.sb("on_r", 1, 128)
    wr_sb = p.sb("wrsb", 128, 8 * D)
    wk_sb = p.sb("wksb", 128, 8 * D)
    wv_sb = p.sb("wvsb", 128, 8 * D)
    murb = p.sb("murb", 128, D)
    mukb = p.sb("mukb", 128, D)
    xa = p.sb("xa", 128, D)
    xb = p.sb("xb", 128, D)
    h = p.sb("h", 128, D)
    hp = p.sb("hp", 128, D)
    scr = p.sb("scr", 128, D)
    dd = p.sb("dd", 128, D)
    xr = p.sb("xr", 128, D)
    xT = p.sb("xT", 128, D)
    r_sb = p.sb("r_sb", 128, D)
    ka_sb = p.sb("ka_sb", 128, D)
    o_sb = p.sb("o_sb", 128, D)
    c1 = p.sb("c1", 128, 1)
    c2 = p.sb("c2", 128, 1)
    pT = p.ps("pT", 128, 128)
    pM = p.ps("pM", 128, 512)

    p.dma(id_sb[:, :], ident)
    p.dma(on_r[:, :], ones_r)
    murow = p.sb("murow", 1, D)
    mukrow = p.sb("mukrow", 1, D)
    p.dma(murow[:, :], mur)
    p.dma(mukrow[:, :], muk)
    p.bcast(murb, murow, D, on_r, pM)
    p.bcast(mukb, mukrow, D, on_r, pM)
    for kt in range(8):
        p.dma(wr_sb[:, kt * D:(kt + 1) * D], wr[kt * 128:(kt + 1) * 128, :])
        p.dma(wk_sb[:, kt * D:(kt + 1) * D], wk[kt * 128:(kt + 1) * 128, :])
        p.dma(wv_sb[:, kt * D:(kt + 1) * D], wv[kt * 128:(kt + 1) * 128, :])

    def matmul3(src, wsb, dst_psum_consumer):
        p.transp(xT, src, D, id_sb, pT)
        for o in range(0, D, 512):
            for kt in range(8):
                p.mm(pM[:, :], xT[:, kt * 128:(kt + 1) * 128],
                     wsb[:, kt * D + o: kt * D + o + 512], kt == 0, kt == 7)
            dst_psum_consumer(o)

    for t in range(16):
        p.dma(xa[:, :], xe[1 + t * 128: 1 + (t + 1) * 128, :])
        p.dma(xb[:, :], xe[t * 128: t * 128 + 128, :])
        p.ln(h[:, :], xa[:, :], D, scr[:, :], c1, c2)
        p.ln(hp[:, :], xb[:, :], D, scr[:, :], c1, c2)
        p.vtt(dd[:, :], h[:, :], hp[:, :], ALU.subtract)
        p.vtt(xr[:, :], dd[:, :], murb[:, :], ALU.mult)
        p.vtt(xr[:, :], xr[:, :], hp[:, :], ALU.add)
        matmul3(xr, wr_sb, lambda o: p.act(r_sb[:, o:o + 512], pM[:, :], AF.Sigmoid))
        p.vtt(xr[:, :], dd[:, :], mukb[:, :], ALU.mult)
        p.vtt(xr[:, :], xr[:, :], hp[:, :], ALU.add)

        def kcons(o):
            p.vts(scr[:, o:o + 512], pM[:, :], 0.0, ALU.max)
            p.act(ka_sb[:, o:o + 512], scr[:, o:o + 512], AF.Square)
        matmul3(xr, wk_sb, kcons)
        matmul3(ka_sb, wv_sb,
                lambda o: p.vtt(o_sb[:, o:o + 512], pM[:, :], r_sb[:, o:o + 512], ALU.mult))
        p.ln(scr[:, :], o_sb[:, :], D, dd[:, :], c1, c2)
        p.vtt(xa[:, :], xa[:, :], scr[:, :], ALU.add)
        p.dma(x2[t * 128:(t + 1) * 128, :], xa[:, :])
    return p.build()


def launch_l0ffn(x1, WrT, WkT, WvT, mu_r, mu_k):
    key = ("l0ffn",)
    ident = f32(np.eye(128))
    ones_r = f32(np.ones((1, 128)))
    in_maps = []
    for c in range(NC):
        r0 = c * RPC
        prev = np.zeros((1, D), np.float32) if r0 % T == 0 else x1[r0 - 1:r0]
        xe = np.concatenate([prev, x1[r0:r0 + RPC]], 0)
        in_maps.append({"xe": f32(xe), "wr": f32(WrT), "wk": f32(WkT), "wv": f32(WvT),
                        "mur": f32(mu_r.reshape(1, D)), "muk": f32(mu_k.reshape(1, D)),
                        "ident": ident, "ones_r": ones_r})
    res = run(key, build_l0ffn, in_maps)
    return np.concatenate([res[c]["x2"] for c in range(NC)], 0)


def build_l1attn():
    p = Prog()
    Q = p.din("Q", (32, T))
    K4 = p.din("K4", (32, 32, 128, 1))
    V4 = p.din("V4", (32, 32, 128, 1))
    ones_r = p.din("ones_r", (1, 128))
    ut = p.din("ut", (128, 128))
    OUT = p.dout("OUT", (32, T))

    on_r = p.sb("on_r", 1, 128)
    ut_sb = p.sb("ut_sb", 128, 128)
    qrow = p.sb("qrow", 1, T)
    qb = p.sb("qb", 128, T)
    kcol = p.sb("kcol", 128, 32)
    kcol8 = p.sb("kcol8", 128, 32)
    vo = p.sb("vo", 128, 64)
    et = p.sb("et", 128, T)
    dscr = p.sb("dscr", 128, 128)
    o2 = p.sb("o2", 2, 512)
    den = p.sb("den", 1, 512)
    rec = p.sb("rec", 1, 512)
    arow = p.sb("arow", 1, T)
    pO = [p.ps(f"pO{i}", 128, 512) for i in range(8)]

    p.dma(on_r[:, :], ones_r)
    p.dma(ut_sb[:, :], ut)
    p.vmemset(vo[:, :], 1.0)

    for pr in range(32):
        p.dma(qrow[:, :], Q[pr:pr + 1, :])
        for ig in range(8):
            p.mm(pO[ig][:, :], on_r[0:1, :], qrow[0:1, ig * 512:(ig + 1) * 512], True, True)
            p.scopy(qb[:, ig * 512:(ig + 1) * 512], pO[ig][:, :])
        for jt in range(32):
            p.dma(kcol[:, jt:jt + 1], K4[pr, jt])
            p.dma(vo[:, 2 * jt:2 * jt + 1], V4[pr, jt])
        p.smul(kcol8[:, :], kcol[:, :], 0.125)
        for jt in range(32):
            j0 = jt * 128
            p.act(dscr[:, :], qb[:, j0:j0 + 128], AF.Exp, scale=kcol8[:, jt:jt + 1])
            p.vtt(et[:, j0:j0 + 128], dscr[:, :], ut_sb[:, :], ALU.mult)
            if j0 + 128 < T:
                p.act(et[:, j0 + 128:T], qb[:, j0 + 128:T], AF.Exp,
                      scale=kcol8[:, jt:jt + 1])
            ig0 = jt // 4
            for ig in range(ig0, 8):
                c0 = max(ig * 512, j0)
                p.mm(pO[ig][0:2, c0 - ig * 512:512], vo[:, 2 * jt:2 * jt + 2],
                     et[:, c0:(ig + 1) * 512], jt == 0, jt == min(4 * ig + 3, 31))
        for ig in range(8):
            p.vcopy(o2[:, :], pO[ig][0:2, :])
            p.dma(den[0:1, :], o2[1:2, :])
            p.vrecip(rec[:, :], den[:, :])
            p.vtt(arow[:, ig * 512:(ig + 1) * 512], o2[0:1, :], rec[0:1, :], ALU.mult)
        p.dma(OUT[pr:pr + 1, :], arow[:, :])
    return p.build()


def launch_l1attn(qkv):
    key = ("l1attn",)
    q3 = qkv.reshape(B, T, 3, 64)
    Qp = np.ascontiguousarray(q3[:, :, 0, :].transpose(0, 2, 1).reshape(256, T))
    Kp = np.ascontiguousarray(q3[:, :, 1, :].transpose(0, 2, 1).reshape(256, T))
    Vp = np.ascontiguousarray(q3[:, :, 2, :].transpose(0, 2, 1).reshape(256, T))
    K4 = Kp.reshape(256, 32, 128, 1)
    V4 = Vp.reshape(256, 32, 128, 1)
    ones_r = f32(np.ones((1, 128)))
    ut = f32(np.triu(np.ones((128, 128))))
    in_maps = []
    for c in range(NC):
        sl = slice(c * 32, (c + 1) * 32)
        in_maps.append({"Q": f32(Qp[sl]), "K4": f32(K4[sl]), "V4": f32(V4[sl]),
                        "ones_r": ones_r, "ut": ut})
    res = run(key, build_l1attn, in_maps)
    out = np.concatenate([res[c]["OUT"] for c in range(NC)], 0)
    out64 = out.reshape(B, 64, T).transpose(0, 2, 1).reshape(ROWS, 64)
    return np.ascontiguousarray(out64)


def build_qkvprep():
    p = Prog()
    ze = p.din("ze", (RPC + 1, D))
    wq = p.din("wq", (D, 1536))
    qs = p.din("qs", (1, 64))
    ks = p.din("ks", (1, 64))
    tcq = p.din("tcq", (RPC, 64))
    tsq = p.din("tsq", (RPC, 64))
    tck = p.din("tck", (RPC, 64))
    tsk = p.din("tsk", (RPC, 64))
    ident = p.din("ident", (128, 128))
    ones_r = p.din("ones_r", (1, 128))
    QR = p.dout("QR", (RPC, 512))
    KR = p.dout("KR", (RPC, 512))
    VO = p.dout("VO", (RPC, 512))

    id_sb = p.sb("idsb", 128, 128)
    on_r = p.sb("on_r", 1, 128)
    w_sb = p.sb("wsb", 128, 8 * 1536)
    qsrow = p.sb("qsrow", 1, 64)
    ksrow = p.sb("ksrow", 1, 64)
    qsb_b = p.sb("qsb_b", 128, 64)
    ksb_b = p.sb("ksb_b", 128, 64)
    hs = p.sb("hs", 128, D)
    hT = p.sb("hT", 128, D)
    qv = p.sb("qv", 128, 512)
    kv = p.sb("kv", 128, 512)
    ct = p.sb("ct", 128, 64)
    st_ = p.sb("st_", 128, 64)
    scr = p.sb("scr", 128, 64)
    nrm = p.sb("nrm", 128, 1)
    rn = p.sb("rn", 128, 1)
    qn = p.sb("qn", 128, 64)
    tmp = p.sb("tmp", 128, 64)
    t1 = p.sb("t1", 128, 64)
    orot = p.sb("orot", 128, 512)
    pT = p.ps("pT", 128, 128)
    pM = p.ps("pM", 128, 512)

    p.dma(id_sb[:, :], ident)
    p.dma(on_r[:, :], ones_r)
    p.dma(qsrow[:, :], qs)
    p.dma(ksrow[:, :], ks)
    p.bcast(qsb_b, qsrow, 64, on_r, pM)
    p.bcast(ksb_b, ksrow, 64, on_r, pM)
    for kt in range(8):
        p.dma(w_sb[:, kt * 1536:(kt + 1) * 1536], wq[kt * 128:(kt + 1) * 128, :])

    for t in range(16):
        p.dma(hs[:, 0:512], ze[1 + t * 128: 1 + (t + 1) * 128, 0:512])
        p.dma(hs[:, 512:D], ze[t * 128: t * 128 + 128, 512:D])
        p.transp(hT, hs, D, id_sb, pT)
        for ch, dst in ((0, qv), (1, kv)):
            for kt in range(8):
                p.mm(pM[:, :], hT[:, kt * 128:(kt + 1) * 128],
                     w_sb[:, kt * 1536 + ch * 512: kt * 1536 + ch * 512 + 512],
                     kt == 0, kt == 7)
            p.scopy(dst[:, :], pM[:, :])
        for kt in range(8):
            p.mm(pM[:, :], hT[:, kt * 128:(kt + 1) * 128],
                 w_sb[:, kt * 1536 + 1024: kt * 1536 + 1536], kt == 0, kt == 7)
        p.scopy(hs[:, 0:512], pM[:, :])
        p.dma(VO[t * 128:(t + 1) * 128, :], hs[:, 0:512])
        for (tab, dst) in ((tcq, ct), (tsq, st_)):
            p.dma(dst[:, :], tab[t * 128:(t + 1) * 128, :])
        for hh in range(8):
            src = qv[:, hh * 64:(hh + 1) * 64]
            p.act(scr[:, :], src, AF.Square)
            p.vred(nrm[:, 0:1], scr[:, :], ALU.add)
            p.act(rn[:, 0:1], nrm[:, 0:1], AF.Sqrt)
            p.vts(nrm[:, 0:1], rn[:, 0:1], 1e-12, ALU.max)
            p.vrecip(rn[:, 0:1], nrm[:, 0:1])
            p.vts(qn[:, :], src, rn[:, 0:1], ALU.mult)
            p.vtt(qn[:, :], qn[:, :], qsb_b[:, :], ALU.mult)
            p.scopy(tmp[:, 0:32], qn[:, 32:64])
            p.scopy(tmp[:, 32:64], qn[:, 0:32])
            p.vtt(t1[:, :], qn[:, :], ct[:, :], ALU.mult)
            p.vtt(tmp[:, :], tmp[:, :], st_[:, :], ALU.mult)
            p.vtt(orot[:, hh * 64:(hh + 1) * 64], t1[:, :], tmp[:, :], ALU.add)
        p.dma(QR[t * 128:(t + 1) * 128, :], orot[:, :])
        for (tab, dst) in ((tck, ct), (tsk, st_)):
            p.dma(dst[:, :], tab[t * 128:(t + 1) * 128, :])
        for hh in range(8):
            src = kv[:, hh * 64:(hh + 1) * 64]
            p.act(scr[:, :], src, AF.Square)
            p.vred(nrm[:, 0:1], scr[:, :], ALU.add)
            p.act(rn[:, 0:1], nrm[:, 0:1], AF.Sqrt)
            p.vts(nrm[:, 0:1], rn[:, 0:1], 1e-12, ALU.max)
            p.vrecip(rn[:, 0:1], nrm[:, 0:1])
            p.vts(qn[:, :], src, rn[:, 0:1], ALU.mult)
            p.vtt(qn[:, :], qn[:, :], ksb_b[:, :], ALU.mult)
            p.scopy(tmp[:, 0:32], qn[:, 32:64])
            p.scopy(tmp[:, 32:64], qn[:, 0:32])
            p.vtt(t1[:, :], qn[:, :], ct[:, :], ALU.mult)
            p.vtt(tmp[:, :], tmp[:, :], st_[:, :], ALU.mult)
            p.vtt(orot[:, hh * 64:(hh + 1) * 64], t1[:, :], tmp[:, :], ALU.add)
        p.dma(KR[t * 128:(t + 1) * 128, :], orot[:, :])
    return p.build()


def _rotary_tables():
    half = np.arange(0, 64, 2, dtype=np.float64)
    inv_freq = 1.0 / 10000.0 ** (half / 64.0)
    tt = np.arange(T, dtype=np.float64)
    freqs = tt[:, None] * inv_freq[None, :]
    freqs = np.concatenate([freqs, freqs], -1)
    base = (half + 0.4 * 64) / (1.4 * 64)
    power = (tt - T // 2) / 256.0
    scale = base[None, :] ** power[:, None]
    scale = np.concatenate([scale, scale], -1)
    cos, sin = np.cos(freqs), np.sin(freqs)
    sgn = np.concatenate([-np.ones(32), np.ones(32)])[None, :]
    tcq = cos * scale * 8.0
    tsq = sin * scale * sgn * 8.0
    tck = cos / scale
    tsk = sin / scale * sgn
    return f32(tcq), f32(tsq), f32(tck), f32(tsk)


def launch_qkvprep(z, wqkv_l, qs_l, ks_l):
    key = ("qkvprep",)
    tcq, tsq, tck, tsk = _rotary_tables()
    ident = f32(np.eye(128))
    ones_r = f32(np.ones((1, 128)))
    in_maps = []
    for c in range(NC):
        r0 = c * RPC
        prev = np.zeros((1, D), np.float32) if r0 % T == 0 else z[r0 - 1:r0]
        ze = np.concatenate([prev, z[r0:r0 + RPC]], 0)
        t0 = r0 % T
        in_maps.append({"ze": f32(ze), "wq": f32(wqkv_l),
                        "qs": f32(qs_l.reshape(1, 64)), "ks": f32(ks_l.reshape(1, 64)),
                        "tcq": tcq[t0:t0 + RPC], "tsq": tsq[t0:t0 + RPC],
                        "tck": tck[t0:t0 + RPC], "tsk": tsk[t0:t0 + RPC],
                        "ident": ident, "ones_r": ones_r})
    res = run(key, build_qkvprep, in_maps)
    qr = np.concatenate([res[c]["QR"] for c in range(NC)], 0)
    kr = np.concatenate([res[c]["KR"] for c in range(NC)], 0)
    vv = np.concatenate([res[c]["VO"] for c in range(NC)], 0)
    return qr, kr, vv


def build_locattn():
    p = Prog()
    QT = p.din("QT", (32, 64, 512))
    KT = p.din("KT", (32, 64, 1024))
    VW = p.din("VW", (32, 1024, 64))
    LM = p.din("LM", (2, 4, 128, 1024))
    ident = p.din("ident", (128, 128))
    O = p.dout("O", (32, 4, 128, 64))

    id_sb = p.sb("idsb", 128, 128)
    lm_sb = p.sb("lmsb", 128, 8 * 1024)
    qt_sb = p.sb("qtsb", 64, 512)
    kt_sb = p.sb("ktsb", 64, 1024)
    vw_sb = p.sb("vwsb", 128, 512)
    sm = p.sb("sm", 128, 1024)
    e_sb = p.sb("e_sb", 128, 1024)
    et_sb = p.sb("et_sb", 128, 1024)
    mc = p.sb("mc", 128, 1)
    nm = p.sb("nm", 128, 1)
    dc = p.sb("dc", 128, 1)
    idv = p.sb("idv", 128, 1)
    o_sb = p.sb("o_sb", 128, 64)
    pS0 = p.ps("pS0", 128, 512)
    pS1 = p.ps("pS1", 128, 512)
    pT = p.ps("pT", 128, 128)
    pO = p.ps("pO", 128, 64)

    p.dma(id_sb[:, :], ident)
    for v in range(2):
        for k in range(4):
            p.dma(lm_sb[:, (v * 4 + k) * 1024:(v * 4 + k + 1) * 1024], LM[v, k])

    for blk in range(32):
        w0 = 1 if blk % 8 == 0 else 0
        p.dma(qt_sb[:, :], QT[blk])
        p.dma(kt_sb[:, :], KT[blk])
        for jt in range(8):
            p.dma(vw_sb[:, jt * 64:(jt + 1) * 64], VW[blk, jt * 128:(jt + 1) * 128, :])
        for ib in range(4):
            p.mm(pS0[:, :], qt_sb[:, ib * 128:(ib + 1) * 128], kt_sb[:, 0:512], True, True)
            p.mm(pS1[:, :], qt_sb[:, ib * 128:(ib + 1) * 128], kt_sb[:, 512:1024], True, True)
            mb = (w0 * 4 + ib) * 1024
            p.vtt(sm[:, 0:512], pS0[:, :], lm_sb[:, mb:mb + 512], ALU.add)
            p.vtt(sm[:, 512:1024], pS1[:, :], lm_sb[:, mb + 512:mb + 1024], ALU.add)
            p.vred(mc[:, 0:1], sm[:, :], ALU.max)
            p.smul(nm[:, 0:1], mc[:, 0:1], -1.0)
            p.act(e_sb[:, :], sm[:, :], AF.Exp, bias=nm[:, 0:1])
            p.vred(dc[:, 0:1], e_sb[:, :], ALU.add)
            for jt in range(8):
                p.tr(pT[:, :], e_sb[:, jt * 128:(jt + 1) * 128], id_sb)
                p.scopy(et_sb[:, jt * 128:(jt + 1) * 128], pT[:, :])
            for jt in range(8):
                p.mm(pO[:, :], et_sb[:, jt * 128:(jt + 1) * 128],
                     vw_sb[:, jt * 64:(jt + 1) * 64], jt == 0, jt == 7)
            p.vrecip(idv[:, 0:1], dc[:, 0:1])
            p.vts(o_sb[:, :], pO[:, :], idv[:, 0:1], ALU.mult)
            p.dma(O[blk, ib], o_sb[:, :])
    return p.build()


def launch_locattn(qr, kr, vv):
    key = ("locattn",)
    q5 = qr.reshape(B, 8, 512, 8, 64)
    k5 = kr.reshape(B, 8, 512, 8, 64)
    v5 = vv.reshape(B, 8, 512, 8, 64)
    QTb = np.ascontiguousarray(q5.transpose(0, 3, 1, 4, 2)).reshape(256, 64, 512)
    K5 = np.ascontiguousarray(k5.transpose(0, 3, 1, 4, 2))
    prev_idx = [0] + list(range(7))
    KTb = np.concatenate([K5[:, :, prev_idx], K5], axis=4).reshape(256, 64, 1024)
    V5 = np.ascontiguousarray(v5.transpose(0, 3, 1, 2, 4))
    VWb = np.concatenate([V5[:, :, prev_idx], V5], axis=3).reshape(256, 1024, 64)
    il = np.arange(128)[:, None]
    lm = np.zeros((2, 4, 128, 1024), np.float32)
    for k in range(4):
        ilg = il + 128 * k
        jl = np.arange(512)[None, :]
        mprev = (jl < ilg)
        mcur = (jl > ilg)
        lm[0, k, :, 0:512] = np.where(mprev, -1e30, 0.0)
        lm[0, k, :, 512:1024] = np.where(mcur, -1e30, 0.0)
        lm[1, k, :, 0:512] = -1e30
        lm[1, k, :, 512:1024] = np.where(mcur, -1e30, 0.0)
    ident = f32(np.eye(128))
    in_maps = []
    for c in range(NC):
        sl = slice(c * 32, (c + 1) * 32)
        in_maps.append({"QT": f32(QTb[sl]), "KT": f32(KTb[sl]), "VW": f32(VWb[sl]),
                        "LM": lm, "ident": ident})
    res = run(key, build_locattn, in_maps)
    ob = np.concatenate([res[c]["O"] for c in range(NC)], 0)
    o_nat = ob.reshape(B, 8, 8, 512, 64).transpose(0, 2, 3, 1, 4).reshape(ROWS, 512)
    return np.ascontiguousarray(o_nat)


NT17 = 17


def build_innerffn(final):
    p = Prog()
    oe = p.din("oe", (RPC + 128, 512))
    ze = p.din("ze", (RPC + 128, D))
    wo = p.din("wo", (512, D))
    lg = p.din("lg", (1, D))
    lb = p.din("lb", (1, D))
    w1 = p.din("w1", (D, 5460))
    b1 = p.din("b1", (5460, 1))
    w2 = p.din("w2", (2730, D))
    b2 = p.din("b2", (1, D))
    zmask = p.din("zmask", (1, 512))
    ident = p.din("ident", (128, 128))
    ones_r = p.din("ones_r", (1, 128))
    if final:
        x3 = p.din("x3", (RPC, D))
        out = p.dout("out", (RPC, D))
    else:
        out = p.dout("out", (RPC, D))
    z1s = p.nc.dram_tensor("z1s", [RPC + 128, D], F32, kind="Internal").ap()

    id_sb = p.sb("idsb", 128, 128)
    on_r = p.sb("on_r", 1, 128)
    wo_sb = p.sb("wosb", 128, 4 * D)
    lgb = p.sb("lgb", 128, D)
    lbb = p.sb("lbb", 128, D)
    b2b = p.sb("b2b", 128, D)
    zm = p.sb("zm", 1, 512)
    row = p.sb("row", 1, D)
    ot = p.sb("ot", 128, 512)
    oT = p.sb("oT", 128, 512)
    zt = p.sb("zt", 128, D)
    hs = p.sb("hs", 128, D)
    hn = p.sb("hn", 128, D)
    scr = p.sb("scr", 128, D)
    c1 = p.sb("c1", 128, 1)
    c2 = p.sb("c2", 128, 1)
    hnT = p.sb("hnT", 128, 8 * RPC)
    y3 = p.sb("y3", 128, 16 * D)
    w1a = p.sb("w1a", 128, 8 * 128)
    w1g = p.sb("w1g", 128, 8 * 128)
    b1a = p.sb("b1a", 128, 1)
    b1g = p.sb("b1g", 128, 1)
    w2c = p.sb("w2c", 128, D)
    glu = p.sb("glu", 128, 512)
    gel = p.sb("gel", 128, 512)
    pT = p.ps("pT", 128, 128)
    pM = p.ps("pM", 128, 512)
    pA = p.ps("pA", 128, 512)
    pG = p.ps("pG", 128, 512)

    p.dma(id_sb[:, :], ident)
    p.dma(on_r[:, :], ones_r)
    p.dma(zm[:, :], zmask)
    for (r, dst) in ((lg, lgb), (lb, lbb), (b2, b2b)):
        p.dma(row[:, :], r)
        p.bcast(dst, row, D, on_r, pM)
    for kt in range(4):
        p.dma(wo_sb[:, kt * D:(kt + 1) * D], wo[kt * 128:(kt + 1) * 128, :])

    for t in range(NT17):
        p.dma(ot[:, :], oe[t * 128:(t + 1) * 128, :])
        p.transp(oT, ot, 512, id_sb, pT)
        p.dma(zt[:, :], ze[t * 128:(t + 1) * 128, :])
        for o in range(0, D, 512):
            for kt in range(4):
                p.mm(pM[:, :], oT[:, kt * 128:(kt + 1) * 128],
                     wo_sb[:, kt * D + o: kt * D + o + 512], kt == 0, kt == 3)
            p.vtt(zt[:, o:o + 512], zt[:, o:o + 512], pM[:, :], ALU.add)
        p.dma(z1s[t * 128:(t + 1) * 128, :], zt[:, :])
    for t in range(16):
        r0 = 128 + t * 128
        p.dma(hs[:, 0:512], z1s[r0:r0 + 128, 0:512])
        p.dma(hs[:, 512:D], z1s[r0 - 1:r0 + 127, 512:D])
        if t == 0:
            p.vtt(hs[0:1, 512:D], hs[0:1, 512:D], zm[0:1, :], ALU.mult)
        p.ln(hn[:, :], hs[:, :], D, scr[:, :], c1, c2)
        p.vtt(hn[:, :], hn[:, :], lgb[:, :], ALU.mult)
        p.vtt(hn[:, :], hn[:, :], lbb[:, :], ALU.add)
        for kt in range(8):
            p.tr(pT[:, :], hn[:, kt * 128:(kt + 1) * 128], id_sb)
            p.scopy(hnT[:, kt * RPC + t * 128: kt * RPC + (t + 1) * 128], pT[:, :])
    for t in range(16):
        p.dma(zt[:, :], z1s[128 + t * 128: 128 + (t + 1) * 128, :])
        p.vtt(y3[:, t * D:(t + 1) * D], zt[:, :], b2b[:, :], ALU.add)
    chunks = []
    c0 = 0
    while c0 < 2730:
        sk = min(128, 2730 - c0)
        chunks.append((c0, sk))
        c0 += sk
    for (cc, sk) in chunks:
        p.dma(w1a[:, 0:8 * sk].rearrange("c (ct n) -> c ct n", ct=8),
              w1[:, cc:cc + sk].rearrange("(ct c) n -> c ct n", c=128))
        p.dma(w1g[:, 0:8 * sk].rearrange("c (ct n) -> c ct n", ct=8),
              w1[:, 2730 + cc:2730 + cc + sk].rearrange("(ct c) n -> c ct n", c=128))
        p.dma(b1a[0:sk, :], b1[cc:cc + sk, :])
        p.dma(b1g[0:sk, :], b1[2730 + cc:2730 + cc + sk, :])
        p.dma(w2c[0:sk, :], w2[cc:cc + sk, :])
        for th in range(4):
            tb0 = th * 512
            for kt in range(8):
                p.mm(pA[0:sk, :], w1a[:, kt * sk:(kt + 1) * sk],
                     hnT[:, kt * RPC + tb0: kt * RPC + tb0 + 512], kt == 0, kt == 7)
                p.mm(pG[0:sk, :], w1g[:, kt * sk:(kt + 1) * sk],
                     hnT[:, kt * RPC + tb0: kt * RPC + tb0 + 512], kt == 0, kt == 7)
            p.act(gel[0:sk, :], pG[0:sk, :], AF.Gelu, bias=b1g[0:sk, 0:1])
            p.vstt(glu[0:sk, :], pA[0:sk, :], b1a[0:sk, 0:1], gel[0:sk, :],
                   ALU.add, ALU.mult)
            for tb in range(4):
                tt = th * 4 + tb
                for o in range(0, D, 512):
                    p.mm(pM[:, :], glu[0:sk, tb * 128:(tb + 1) * 128],
                         w2c[0:sk, o:o + 512], True, True)
                    p.vtt(y3[:, tt * D + o: tt * D + o + 512],
                          y3[:, tt * D + o: tt * D + o + 512], pM[:, :], ALU.add)
    for t in range(16):
        if final:
            p.ln(hn[:, :], y3[:, t * D:(t + 1) * D], D, scr[:, :], c1, c2)
            p.dma(zt[:, :], x3[t * 128:(t + 1) * 128, :])
            p.vtt(hn[:, :], hn[:, :], zt[:, :], ALU.add)
            p.dma(out[t * 128:(t + 1) * 128, :], hn[:, :])
        else:
            p.scopy(hn[:, :], y3[:, t * D:(t + 1) * D])
            p.dma(out[t * 128:(t + 1) * 128, :], hn[:, :])
    return p.build()


def launch_innerffn(z, o_nat, wo_l, lg_l, lb_l, w1_l, b1_l, w2_l, b2_l,
                    final=False, x3=None):
    key = ("innerffn", final)
    ident = f32(np.eye(128))
    ones_r = f32(np.ones((1, 128)))
    in_maps = []
    for c in range(NC):
        r0 = c * RPC
        if r0 == 0:
            pz = np.zeros((128, D), np.float32)
            po = np.zeros((128, 512), np.float32)
        else:
            pz = z[r0 - 128:r0]
            po = o_nat[r0 - 128:r0]
        zmask = np.zeros((1, 512), np.float32) if r0 % T == 0 else np.ones((1, 512), np.float32)
        m = {"oe": f32(np.concatenate([po, o_nat[r0:r0 + RPC]], 0)),
             "ze": f32(np.concatenate([pz, z[r0:r0 + RPC]], 0)),
             "wo": f32(wo_l), "lg": f32(lg_l.reshape(1, D)), "lb": f32(lb_l.reshape(1, D)),
             "w1": f32(w1_l), "b1": f32(b1_l.reshape(5460, 1)), "w2": f32(w2_l),
             "b2": f32(b2_l.reshape(1, D)), "zmask": zmask,
             "ident": ident, "ones_r": ones_r}
        if final:
            m["x3"] = f32(x3[r0:r0 + RPC])
        in_maps.append(m)
    res = run(key, lambda: build_innerffn(final), in_maps)
    return np.concatenate([res[c]["out"] for c in range(NC)], 0)


def kernel(L0_attn_x, L0_attn_Wq, L0_attn_Wk, L0_attn_Wv, L0_attn_Wout, L0_attn_betas,
           L0_ffn_W_r, L0_ffn_W_k, L0_ffn_W_v, L0_ffn_mu_r, L0_ffn_mu_k,
           L1_attn_w_qkv, L1_attn_w_out, L1_ffn_wqkv, L1_ffn_q_scale, L1_ffn_k_scale,
           L1_ffn_wo, L1_ffn_ln_g, L1_ffn_ln_b, L1_ffn_w1, L1_ffn_b1, L1_ffn_w2,
           L1_ffn_b2):
    x0 = f32(L0_attn_x).reshape(ROWS, D)
    qf, kf, vf = launch_lnproj(x0, [f32(L0_attn_Wq).T, f32(L0_attn_Wk).T,
                                    f32(L0_attn_Wv).T])
    att = launch_l0attn(qf, kf, vf, f32(L0_attn_betas))
    x1 = launch_projres(att, f32(L0_attn_Wout).T, x0, out_z=False)
    x2 = launch_l0ffn(x1, f32(L0_ffn_W_r).T, f32(L0_ffn_W_k).T, f32(L0_ffn_W_v).T,
                      f32(L0_ffn_mu_r), f32(L0_ffn_mu_k))
    (qkv,) = launch_lnproj(x2, [f32(L1_attn_w_qkv).T])
    out64 = launch_l1attn(qkv)
    x3, z = launch_projres(out64, f32(L1_attn_w_out).T, x2, out_z=True)
    cur = z
    for l in range(2):
        qr, kr, vv = launch_qkvprep(cur, f32(L1_ffn_wqkv)[l], f32(L1_ffn_q_scale)[l],
                                    f32(L1_ffn_k_scale)[l])
        o_nat = launch_locattn(qr, kr, vv)
        cur = launch_innerffn(cur, o_nat, f32(L1_ffn_wo)[l], f32(L1_ffn_ln_g)[l],
                              f32(L1_ffn_ln_b)[l], f32(L1_ffn_w1)[l], f32(L1_ffn_b1)[l],
                              f32(L1_ffn_w2)[l], f32(L1_ffn_b2)[l],
                              final=(l == 1), x3=x3)
    return cur.reshape(B, T, D).astype(np.float32)


# version 02
# speedup vs baseline: 1.2905x; 1.2905x over previous
import numpy as np
import concourse.bass as bass
import concourse.mybir as mybir
from concourse.bass_utils import run_bass_kernel_spmd

F32 = mybir.dt.float32
AF = mybir.ActivationFunctionType
ALU = mybir.AluOpType
AX = mybir.AxisListType

NC = 8
B = 4
T = 4096
D = 1024
ROWS = B * T
RPC = ROWS // NC
EPS = 1e-5


def f32(a):
    return np.ascontiguousarray(np.asarray(a), dtype=np.float32)


class Prog:

    def __init__(self):
        self.nc = bass.Bass("TRN2", target_bir_lowering=False)
        self.items = []
        self.consts = {}
        self._gid = None
        self._gctr = 0

    def beg(self):
        self._gctr += 1
        self._gid = self._gctr

    def end(self):
        self._gid = None

    def din(self, name, shape):
        return self.nc.dram_tensor(name, list(shape), F32, kind="ExternalInput").ap()

    def dout(self, name, shape):
        return self.nc.dram_tensor(name, list(shape), F32, kind="ExternalOutput").ap()

    def sb(self, name, p, f):
        return self.nc.alloc_sbuf_tensor(name, [p, f], F32)

    def ps(self, name, p, f):
        return self.nc.alloc_psum_tensor(name, [p, f], F32)

    def _e(self, eng, meth, *args, inc=1, **kw):
        self.items.append((eng, meth, args, kw, inc, self._gid))

    def dma(self, out, in_, **kw):
        self._e("sync", "dma_start", inc=16, out=out, in_=in_, **kw)

    def mm(self, out, lhsT, rhs, first, last):
        self._e("tensor", "matmul", out, lhsT, rhs, start=first, stop=last,
                skip_group_check=True)

    def tr(self, out_psum, in_, ident):
        if not isinstance(ident, bass.AP):
            ident = ident[:, :]
        self._e("tensor", "transpose", out_psum, in_, ident)

    def act(self, out, in_, func, bias=0.0, scale=1.0):
        for v in (bias, scale):
            if isinstance(v, float) and v not in (0.0, 1.0):
                self.consts.setdefault(v, len(self.consts))
        self._e("scalar", "activation", out, in_, func, bias=bias, scale=scale)

    def scopy(self, out, in_):
        self._e("scalar", "copy", out, in_)

    def smul(self, out, in_, c):
        self._e("scalar", "mul", out, in_, c)

    def vtt(self, out, a, b, op):
        self._e("vector", "tensor_tensor", out, a, b, op=op)

    def vts(self, out, in_, scalar, op):
        self._e("vector", "tensor_single_scalar", out, in_, scalar, op)

    def vstt(self, out, in0, scalar, in1, op0, op1):
        self._e("vector", "scalar_tensor_tensor", out, in0, scalar, in1, op0, op1)

    def vred(self, out, in_, op):
        self._e("vector", "tensor_reduce", out, in_, AX.X, op)

    def vrecip(self, out, in_):
        self._e("vector", "reciprocal", out, in_)

    def vmemset(self, ap, c):
        self._e("vector", "memset", ap, c)

    def vcopy(self, out, in_):
        self._e("vector", "tensor_copy", out, in_)

    def ln(self, dst, src, W, scr, c1, c2, P=128):
        self.vred(c1[0:P, 0:1], src, ALU.add)
        self.smul(c2[0:P, 0:1], c1[0:P, 0:1], 1.0 / W)
        self.vts(scr, src, c2[0:P, 0:1], ALU.subtract)
        self.act(dst, scr, AF.Square)
        self.vred(c1[0:P, 0:1], dst, ALU.add)
        self.act(c2[0:P, 0:1], c1[0:P, 0:1], AF.Sqrt, bias=EPS, scale=1.0 / W)
        self.vrecip(c1[0:P, 0:1], c2[0:P, 0:1])
        self.vts(dst, scr, c1[0:P, 0:1], ALU.mult)

    def bcast(self, dst, row, W, ones_r, pscr):
        for o in range(0, W, 512):
            w = min(512, W - o)
            self.mm(pscr[:, 0:w], ones_r[0:1, 0:128], row[0:1, o:o + w], True, True)
            self.scopy(dst[:, o:o + w], pscr[:, 0:w])

    def transp(self, dst, src, W, ident, pscr):
        for o in range(0, W, 128):
            w = min(128, W - o)
            self.tr(pscr[0:w, 0:128], src[:, o:o + w], ident)
            self.scopy(dst[0:w, o:o + 128], pscr[0:w, 0:128])

    def build(self):
        nc = self.nc
        if self.consts:
            ctile = nc.alloc_sbuf_tensor("cconsts", [128, len(self.consts)], F32)
            pre = []
            for v, i in self.consts.items():
                nc.const_aps.register(F32, v, ctile[:, i:i + 1])
                pre.append(("vector", "memset", (ctile[:, i:i + 1], v), {}, 1, None))
            self.items = pre + self.items
        waits = []
        cum = 0
        prev_g = object()
        gstart = 0
        for it in self.items:
            g = it[5]
            if g is None or g != prev_g:
                gstart = cum
            prev_g = g if g is not None else object()
            waits.append(gstart)
            cum += it[4]
        total = cum
        used = sorted(set(it[0] for it in self.items))
        with nc.semaphore() as S, nc.Block() as blk:
            def mk(name):
                def body(eng):
                    last_wait = 0
                    for (en, meth, args, kw, inc, g), w in zip(self.items, waits):
                        if en == name:
                            if w > last_wait:
                                eng.wait_ge(S, w)
                                last_wait = w
                            getattr(eng, meth)(*args, **kw).then_inc(S, inc)
                    eng.wait_ge(S, total)
                return body
            for name in used:
                getattr(blk, name)(mk(name))
        return nc


_cache = {}


def run(key, buildfn, in_maps):
    if key not in _cache:
        _cache[key] = buildfn()
    nc = _cache[key]
    res = run_bass_kernel_spmd(nc, in_maps, core_ids=list(range(NC)))
    return res.results


def build_lnproj(widths, pre_ln=True):
    p = Prog()
    x = p.din("x", (RPC, D))
    ws = [p.din(f"w{i}", (D, w)) for i, w in enumerate(widths)]
    outs = [p.dout(f"o{i}", (RPC, w)) for i, w in enumerate(widths)]
    ident = p.din("ident", (128, 128))

    w_sb = [p.sb(f"w{i}sb", 128, 8 * w) for i, w in enumerate(widths)]
    id_sb = p.sb("idsb", 128, 128)
    xt = p.sb("xt", 128, D)
    scr = p.sb("scr", 128, D)
    h = p.sb("h", 128, D)
    hT = p.sb("hT", 128, D)
    c1 = p.sb("c1", 128, 1)
    c2 = p.sb("c2", 128, 1)
    pT = p.ps("pT", 128, 128)
    pM = p.ps("pM", 128, 512)

    p.beg()
    p.dma(id_sb[:, :], ident)
    for i, w in enumerate(widths):
        for kt in range(8):
            p.dma(w_sb[i][:, kt * w:(kt + 1) * w], ws[i][kt * 128:(kt + 1) * 128, :])
    p.end()

    for t in range(16):
        p.dma(xt[:, :], x[t * 128:(t + 1) * 128, :])
        if pre_ln:
            p.ln(h[:, :], xt[:, :], D, scr[:, :], c1, c2)
        else:
            p.scopy(h[:, :], xt[:, :])
        p.transp(hT, h, D, id_sb, pT)
        for i, w in enumerate(widths):
            for o in range(0, w, 512):
                n = min(512, w - o)
                p.beg()
                for kt in range(8):
                    p.mm(pM[:, 0:n], hT[:, kt * 128:(kt + 1) * 128],
                         w_sb[i][:, kt * w + o: kt * w + o + n], kt == 0, kt == 7)
                p.end()
                p.scopy(scr[:, 0:n], pM[:, 0:n])
                p.dma(outs[i][t * 128:(t + 1) * 128, o:o + n], scr[:, 0:n])
    return p.build()


def launch_lnproj(xfull, weights, pre_ln=True, tag=""):
    widths = [w.shape[1] for w in weights]
    key = ("lnproj", tuple(widths), pre_ln)
    ident = f32(np.eye(128))
    in_maps = []
    for c in range(NC):
        m = {"x": f32(xfull[c * RPC:(c + 1) * RPC]), "ident": ident}
        for i, w in enumerate(weights):
            m[f"w{i}"] = f32(w)
        in_maps.append(m)
    res = run(key, lambda: build_lnproj(widths, pre_ln), in_maps)
    return [np.concatenate([res[c][f"o{i}"] for c in range(NC)], 0)
            for i in range(len(widths))]


def build_projres(K, out_z):
    p = Prog()
    a = p.din("a", (RPC, K))
    w = p.din("w", (K, D))
    x = p.din("x", (RPC, D))
    xo = p.dout("xo", (RPC, D))
    z = p.dout("z", (RPC, D)) if out_z else None
    ident = p.din("ident", (128, 128))

    nk = K // 128 if K >= 128 else 1
    w_sb = p.sb("wsb", 128 if K >= 128 else K, nk * D if K >= 128 else D)
    id_sb = p.sb("idsb", 128, 128)
    at = p.sb("at", 128, K)
    aT = p.sb("aT", 128 if K >= 128 else K, 128 * (nk if K >= 128 else 1))
    y = p.sb("y", 128, D)
    scr = p.sb("scr", 128, D)
    xt = p.sb("xt", 128, D)
    c1 = p.sb("c1", 128, 1)
    c2 = p.sb("c2", 128, 1)
    pT = p.ps("pT", 128, 128)
    pM = p.ps("pM", 128, 512)

    p.beg()
    p.dma(id_sb[:, :], ident)
    if K >= 128:
        for kt in range(nk):
            p.dma(w_sb[:, kt * D:(kt + 1) * D], w[kt * 128:(kt + 1) * 128, :])
    else:
        p.dma(w_sb[:, :], w)
    p.end()

    for t in range(16):
        p.dma(at[:, 0:K], a[t * 128:(t + 1) * 128, :])
        if K >= 128:
            for kt in range(nk):
                p.tr(pT[:, :], at[:, kt * 128:(kt + 1) * 128], id_sb)
                p.scopy(aT[:, kt * 128:(kt + 1) * 128], pT[:, :])
        else:
            p.tr(pT[0:K, 0:128], at[:, 0:K], id_sb)
            p.scopy(aT[0:K, 0:128], pT[0:K, 0:128])
        for o in range(0, D, 512):
            if K >= 128:
                p.beg()
                for kt in range(nk):
                    p.mm(pM[:, :], aT[:, kt * 128:(kt + 1) * 128],
                         w_sb[:, kt * D + o: kt * D + o + 512], kt == 0, kt == nk - 1)
                p.end()
            else:
                p.mm(pM[:, :], aT[0:K, 0:128], w_sb[0:K, o:o + 512], True, True)
            p.scopy(y[:, o:o + 512], pM[:, :])
        p.ln(y[:, :], y[:, :], D, scr[:, :], c1, c2)
        p.dma(xt[:, :], x[t * 128:(t + 1) * 128, :])
        p.vtt(xt[:, :], xt[:, :], y[:, :], ALU.add)
        p.dma(xo[t * 128:(t + 1) * 128, :], xt[:, :])
        if out_z:
            p.ln(y[:, :], xt[:, :], D, scr[:, :], c1, c2)
            p.dma(z[t * 128:(t + 1) * 128, :], y[:, :])
    return p.build()


def launch_projres(afull, w, xfull, out_z, tag=""):
    K = afull.shape[1]
    key = ("projres", K, out_z)
    ident = f32(np.eye(128))
    in_maps = [{"a": f32(afull[c * RPC:(c + 1) * RPC]), "w": f32(w),
                "x": f32(xfull[c * RPC:(c + 1) * RPC]), "ident": ident}
               for c in range(NC)]
    res = run(key, lambda: build_projres(K, out_z), in_maps)
    xo = np.concatenate([res[c]["xo"] for c in range(NC)], 0)
    if out_z:
        zf = np.concatenate([res[c]["z"] for c in range(NC)], 0)
        return xo, zf
    return xo


def build_l0attn():
    p = Prog()
    QT = p.din("QT", (8, 8, 64, 512))
    KT = p.din("KT", (8, 8, 64, 512))
    Q = p.din("Q", (8, 8, 512, 64))
    K_ = p.din("K", (8, 8, 512, 64))
    V = p.din("V", (8, 8, 512, 64))
    BET = p.din("BET", (8, 64))
    ones_r = p.din("ones_r", (1, 128))
    ones_c = p.din("ones_c", (128, 1))
    ATT = p.dout("ATT", (8, 8, 512, 64))

    on_r = p.sb("on_r", 1, 128)
    on_c = p.sb("on_c", 128, 1)
    qt_s = p.sb("qt_s", 64, 512)
    kt_s = p.sb("kt_s", 64, 512)
    q_sb = p.sb("q_sb", 128, 256)
    k_sb = p.sb("k_sb", 128, 256)
    vo = p.sb("vo", 128, 4 * 65)
    ska = p.sb("ska", 128, 256)
    sqa = p.sb("sqa", 128, 256)
    sqT = p.sb("sqT", 64, 512)
    tA = p.sb("tA", 128, 512)
    tB = p.sb("tB", 128, 512)
    et = p.sb("et", 128, 2048)
    rs = p.sb("rs", 128, 4)
    irs = p.sb("irs", 128, 4)
    mem = p.sb("mem", 64, 64)
    beta = p.sb("beta", 1, 64)
    gate = p.sb("gate", 1, 64)
    r0r = p.sb("r0r", 1, 64)
    r1r = p.sb("r1r", 1, 64)
    rskr = p.sb("rskr", 1, 64)
    rb = p.sb("rb", 128, 128)
    invd = p.sb("invd", 128, 1)
    t1 = p.sb("t1", 128, 64)
    t2 = p.sb("t2", 128, 64)
    att = p.sb("att", 128, 64)

    pM = p.ps("pM", 64, 64)
    pZ = p.ps("pZ", 1, 64)
    pS = p.ps("pS", 128, 512)
    pA0 = p.ps("pA0", 128, 130)
    pA1 = p.ps("pA1", 128, 130)
    pB = p.ps("pB", 128, 128)
    pN = p.ps("pN", 128, 64)

    p.dma(on_r[:, :], ones_r)
    p.dma(on_c[:, :], ones_c)
    for st in range(4):
        p.vmemset(vo[:, st * 65 + 64: st * 65 + 65], 1.0)

    for pr in range(8):
        p.dma(beta[:, :], BET[pr:pr + 1, :])
        p.act(gate[:, :], beta[:, :], AF.Sigmoid)
        p.vtt(r0r[:, :], on_r[0:1, 0:64], gate[:, :], ALU.subtract)
        p.vmemset(mem[:, :], 0.0)
        for sg in range(8):
            p.beg()
            p.dma(qt_s[:, :], QT[pr, sg])
            p.dma(kt_s[:, :], KT[pr, sg])
            for st in range(4):
                p.dma(q_sb[:, st * 64:(st + 1) * 64], Q[pr, sg, st * 128:(st + 1) * 128, :])
                p.dma(k_sb[:, st * 64:(st + 1) * 64], K_[pr, sg, st * 128:(st + 1) * 128, :])
                p.dma(vo[:, st * 65: st * 65 + 64], V[pr, sg, st * 128:(st + 1) * 128, :])
            p.end()
            if sg > 0:
                p.scopy(mem[:, :], pM[:, :])
            p.vts(tA[:, 0:256], k_sb[:, :], 0.0, ALU.min)
            p.act(tB[:, 0:256], tA[:, 0:256], AF.Exp)
            p.vts(tA[:, 0:256], k_sb[:, :], 0.0, ALU.max)
            p.vtt(ska[:, :], tA[:, 0:256], tB[:, 0:256], ALU.add)
            p.beg()
            for st in range(4):
                p.mm(pM[:, :], ska[:, st * 64:(st + 1) * 64],
                     vo[:, st * 65: st * 65 + 64], sg == 0 and st == 0, st == 3)
                p.mm(pZ[:, :], on_c[:, 0:1], ska[:, st * 64:(st + 1) * 64],
                     sg == 0 and st == 0, st == 3)
            p.end()
            for sjt in range(4):
                p.mm(pS[:, :], kt_s[:, sjt * 128:(sjt + 1) * 128], qt_s[:, :], True, True)
                p.act(et[:, sjt * 512:(sjt + 1) * 512], pS[:, :], AF.Exp, scale=0.125)
            p.vts(tA[0:64, 0:512], qt_s[:, :], 0.0, ALU.min)
            p.act(tB[0:64, 0:512], tA[0:64, 0:512], AF.Exp)
            p.vts(tA[0:64, 0:512], qt_s[:, :], 0.0, ALU.max)
            p.vtt(sqT[:, :], tA[0:64, 0:512], tB[0:64, 0:512], ALU.add)
            p.vts(tA[:, 0:256], q_sb[:, :], 0.0, ALU.min)
            p.act(tB[:, 0:256], tA[:, 0:256], AF.Exp)
            p.vts(tA[:, 0:256], q_sb[:, :], 0.0, ALU.max)
            p.vtt(sqa[:, :], tA[:, 0:256], tB[:, 0:256], ALU.add)
            for st in range(4):
                p.vred(rs[:, st:st + 1], sqa[:, st * 64:(st + 1) * 64], ALU.add)
            p.vrecip(irs[:, :], rs[:, :])
            p.vrecip(rskr[:, :], pZ[0:1, :])
            p.vtt(r1r[:, :], gate[:, :], rskr[:, :], ALU.mult)
            p.mm(pB[:, 0:64], on_r[0:1, :], r1r[0:1, :], True, True)
            p.mm(pB[:, 64:128], on_r[0:1, :], r0r[0:1, :], True, True)
            p.scopy(rb[:, :], pB[:, :])
            for sit in range(4):
                pa = pA0 if sit < 2 else pA1
                sl = (sit % 2) * 65
                p.beg()
                for sjt in range(4):
                    p.mm(pa[:, sl:sl + 65],
                         et[:, sjt * 512 + sit * 128: sjt * 512 + (sit + 1) * 128],
                         vo[:, sjt * 65:(sjt + 1) * 65], sjt == 0, sjt == 3)
                p.mm(pN[:, :], sqT[:, sit * 128:(sit + 1) * 128], mem[:, :], True, True)
                p.end()
                p.vrecip(invd[:, :], pa[:, sl + 64:sl + 65])
                p.vts(t2[:, :], pa[:, sl:sl + 64], invd[:, 0:1], ALU.mult)
                p.vtt(t2[:, :], t2[:, :], rb[:, 64:128], ALU.mult)
                p.vts(t1[:, :], pN[:, :], irs[:, sit:sit + 1], ALU.mult)
                p.vtt(t1[:, :], t1[:, :], rb[:, 0:64], ALU.mult)
                p.vtt(att[:, :], t1[:, :], t2[:, :], ALU.add)
                p.dma(ATT[pr, sg, sit * 128:(sit + 1) * 128, :], att[:, :])
    return p.build()


def launch_l0attn(qf, kf, vf, betas):
    key = ("l0attn",)
    def bands(m):
        return np.ascontiguousarray(
            m.reshape(B, 8, 16, 32, D).transpose(0, 2, 1, 3, 4).reshape(64, 8, 512, 64))
    Qa, Ka, Va = bands(qf), bands(kf), bands(vf)
    QTa = np.ascontiguousarray(Qa.transpose(0, 1, 3, 2))
    KTa = np.ascontiguousarray(Ka.transpose(0, 1, 3, 2))
    bet = np.asarray(betas).reshape(16, 64)
    betp = np.repeat(bet[None, :, :], B, 0).reshape(64, 64)
    ones_r = f32(np.ones((1, 128)))
    ones_c = f32(np.ones((128, 1)))
    in_maps = []
    for c in range(NC):
        sl = slice(c * 8, (c + 1) * 8)
        in_maps.append({"QT": f32(QTa[sl]), "KT": f32(KTa[sl]), "Q": f32(Qa[sl]),
                        "K": f32(Ka[sl]), "V": f32(Va[sl]), "BET": f32(betp[sl]),
                        "ones_r": ones_r, "ones_c": ones_c})
    res = run(key, build_l0attn, in_maps)
    att = np.concatenate([res[c]["ATT"] for c in range(NC)], 0)
    att_nat = att.reshape(B, 16, 8, 32, 16, 64).transpose(0, 2, 1, 3, 4, 5).reshape(ROWS, D)
    return np.ascontiguousarray(att_nat)


def build_l0ffn():
    p = Prog()
    xe = p.din("xe", (RPC + 1, D))
    wr = p.din("wr", (D, D))
    wk = p.din("wk", (D, D))
    wv = p.din("wv", (D, D))
    mur = p.din("mur", (1, D))
    muk = p.din("muk", (1, D))
    ident = p.din("ident", (128, 128))
    ones_r = p.din("ones_r", (1, 128))
    x2 = p.dout("x2", (RPC, D))

    id_sb = p.sb("idsb", 128, 128)
    on_r = p.sb("on_r", 1, 128)
    wr_sb = p.sb("wrsb", 128, 8 * D)
    wk_sb = p.sb("wksb", 128, 8 * D)
    wv_sb = p.sb("wvsb", 128, 8 * D)
    murb = p.sb("murb", 128, D)
    mukb = p.sb("mukb", 128, D)
    xa = p.sb("xa", 128, D)
    xb = p.sb("xb", 128, D)
    h = p.sb("h", 128, D)
    hp = p.sb("hp", 128, D)
    scr = p.sb("scr", 128, D)
    dd = p.sb("dd", 128, D)
    xr = p.sb("xr", 128, D)
    xT = p.sb("xT", 128, D)
    r_sb = p.sb("r_sb", 128, D)
    ka_sb = p.sb("ka_sb", 128, D)
    o_sb = p.sb("o_sb", 128, D)
    c1 = p.sb("c1", 128, 1)
    c2 = p.sb("c2", 128, 1)
    pT = p.ps("pT", 128, 128)
    pM = p.ps("pM", 128, 512)

    murow = p.sb("murow", 1, D)
    mukrow = p.sb("mukrow", 1, D)
    p.beg()
    p.dma(id_sb[:, :], ident)
    p.dma(on_r[:, :], ones_r)
    p.dma(murow[:, :], mur)
    p.dma(mukrow[:, :], muk)
    p.end()
    p.bcast(murb, murow, D, on_r, pM)
    p.bcast(mukb, mukrow, D, on_r, pM)
    p.beg()
    for kt in range(8):
        p.dma(wr_sb[:, kt * D:(kt + 1) * D], wr[kt * 128:(kt + 1) * 128, :])
        p.dma(wk_sb[:, kt * D:(kt + 1) * D], wk[kt * 128:(kt + 1) * 128, :])
        p.dma(wv_sb[:, kt * D:(kt + 1) * D], wv[kt * 128:(kt + 1) * 128, :])
    p.end()

    def matmul3(src, wsb, dst_psum_consumer):
        p.transp(xT, src, D, id_sb, pT)
        for o in range(0, D, 512):
            p.beg()
            for kt in range(8):
                p.mm(pM[:, :], xT[:, kt * 128:(kt + 1) * 128],
                     wsb[:, kt * D + o: kt * D + o + 512], kt == 0, kt == 7)
            p.end()
            dst_psum_consumer(o)

    for t in range(16):
        p.beg()
        p.dma(xa[:, :], xe[1 + t * 128: 1 + (t + 1) * 128, :])
        p.dma(xb[:, :], xe[t * 128: t * 128 + 128, :])
        p.end()
        p.ln(h[:, :], xa[:, :], D, scr[:, :], c1, c2)
        p.ln(hp[:, :], xb[:, :], D, scr[:, :], c1, c2)
        p.vtt(dd[:, :], h[:, :], hp[:, :], ALU.subtract)
        p.vtt(xr[:, :], dd[:, :], murb[:, :], ALU.mult)
        p.vtt(xr[:, :], xr[:, :], hp[:, :], ALU.add)
        matmul3(xr, wr_sb, lambda o: p.act(r_sb[:, o:o + 512], pM[:, :], AF.Sigmoid))
        p.vtt(xr[:, :], dd[:, :], mukb[:, :], ALU.mult)
        p.vtt(xr[:, :], xr[:, :], hp[:, :], ALU.add)

        def kcons(o):
            p.vts(scr[:, o:o + 512], pM[:, :], 0.0, ALU.max)
            p.act(ka_sb[:, o:o + 512], scr[:, o:o + 512], AF.Square)
        matmul3(xr, wk_sb, kcons)
        matmul3(ka_sb, wv_sb,
                lambda o: p.vtt(o_sb[:, o:o + 512], pM[:, :], r_sb[:, o:o + 512], ALU.mult))
        p.ln(scr[:, :], o_sb[:, :], D, dd[:, :], c1, c2)
        p.vtt(xa[:, :], xa[:, :], scr[:, :], ALU.add)
        p.dma(x2[t * 128:(t + 1) * 128, :], xa[:, :])
    return p.build()


def launch_l0ffn(x1, WrT, WkT, WvT, mu_r, mu_k):
    key = ("l0ffn",)
    ident = f32(np.eye(128))
    ones_r = f32(np.ones((1, 128)))
    in_maps = []
    for c in range(NC):
        r0 = c * RPC
        prev = np.zeros((1, D), np.float32) if r0 % T == 0 else x1[r0 - 1:r0]
        xe = np.concatenate([prev, x1[r0:r0 + RPC]], 0)
        in_maps.append({"xe": f32(xe), "wr": f32(WrT), "wk": f32(WkT), "wv": f32(WvT),
                        "mur": f32(mu_r.reshape(1, D)), "muk": f32(mu_k.reshape(1, D)),
                        "ident": ident, "ones_r": ones_r})
    res = run(key, build_l0ffn, in_maps)
    return np.concatenate([res[c]["x2"] for c in range(NC)], 0)


def build_l1attn():
    p = Prog()
    Q = p.din("Q", (32, T))
    K4 = p.din("K4", (32, 32, 128, 1))
    V4 = p.din("V4", (32, 32, 128, 1))
    ones_r = p.din("ones_r", (1, 128))
    ut = p.din("ut", (128, 128))
    OUT = p.dout("OUT", (32, T))

    on_r = p.sb("on_r", 1, 128)
    ut_sb = p.sb("ut_sb", 128, 128)
    qrow = p.sb("qrow", 1, T)
    qb = p.sb("qb", 128, T)
    kcol = p.sb("kcol", 128, 32)
    kcol8 = p.sb("kcol8", 128, 32)
    vo = p.sb("vo", 128, 64)
    et = p.sb("et", 128, T)
    dscr = p.sb("dscr", 128, 128)
    o2 = p.sb("o2", 2, 512)
    den = p.sb("den", 1, 512)
    rec = p.sb("rec", 1, 512)
    arow = p.sb("arow", 1, T)
    pO = [p.ps(f"pO{i}", 128, 512) for i in range(8)]

    p.dma(on_r[:, :], ones_r)
    p.dma(ut_sb[:, :], ut)
    p.vmemset(vo[:, :], 1.0)

    for pr in range(32):
        p.dma(qrow[:, :], Q[pr:pr + 1, :])
        p.beg()
        for ig in range(8):
            p.mm(pO[ig][:, :], on_r[0:1, :], qrow[0:1, ig * 512:(ig + 1) * 512], True, True)
        p.end()
        p.beg()
        for ig in range(8):
            p.scopy(qb[:, ig * 512:(ig + 1) * 512], pO[ig][:, :])
        p.end()
        p.beg()
        for jt in range(32):
            p.dma(kcol[:, jt:jt + 1], K4[pr, jt])
            p.dma(vo[:, 2 * jt:2 * jt + 1], V4[pr, jt])
        p.end()
        p.smul(kcol8[:, :], kcol[:, :], 0.125)
        for jt in range(32):
            j0 = jt * 128
            p.beg()
            p.act(dscr[:, :], qb[:, j0:j0 + 128], AF.Exp, scale=kcol8[:, jt:jt + 1])
            if j0 + 128 < T:
                p.act(et[:, j0 + 128:T], qb[:, j0 + 128:T], AF.Exp,
                      scale=kcol8[:, jt:jt + 1])
            p.end()
            p.vtt(et[:, j0:j0 + 128], dscr[:, :], ut_sb[:, :], ALU.mult)
            ig0 = jt // 4
            p.beg()
            for ig in range(ig0, 8):
                c0 = max(ig * 512, j0)
                p.mm(pO[ig][0:2, c0 - ig * 512:512], vo[:, 2 * jt:2 * jt + 2],
                     et[:, c0:(ig + 1) * 512], jt == 0, jt == min(4 * ig + 3, 31))
            p.end()
        for ig in range(8):
            p.vcopy(o2[:, :], pO[ig][0:2, :])
            p.dma(den[0:1, :], o2[1:2, :])
            p.vrecip(rec[:, :], den[:, :])
            p.vtt(arow[:, ig * 512:(ig + 1) * 512], o2[0:1, :], rec[0:1, :], ALU.mult)
        p.dma(OUT[pr:pr + 1, :], arow[:, :])
    return p.build()


def launch_l1attn(qkv):
    key = ("l1attn",)
    q3 = qkv.reshape(B, T, 3, 64)
    Qp = np.ascontiguousarray(q3[:, :, 0, :].transpose(0, 2, 1).reshape(256, T))
    Kp = np.ascontiguousarray(q3[:, :, 1, :].transpose(0, 2, 1).reshape(256, T))
    Vp = np.ascontiguousarray(q3[:, :, 2, :].transpose(0, 2, 1).reshape(256, T))
    K4 = Kp.reshape(256, 32, 128, 1)
    V4 = Vp.reshape(256, 32, 128, 1)
    ones_r = f32(np.ones((1, 128)))
    ut = f32(np.triu(np.ones((128, 128))))
    in_maps = []
    for c in range(NC):
        sl = slice(c * 32, (c + 1) * 32)
        in_maps.append({"Q": f32(Qp[sl]), "K4": f32(K4[sl]), "V4": f32(V4[sl]),
                        "ones_r": ones_r, "ut": ut})
    res = run(key, build_l1attn, in_maps)
    out = np.concatenate([res[c]["OUT"] for c in range(NC)], 0)
    out64 = out.reshape(B, 64, T).transpose(0, 2, 1).reshape(ROWS, 64)
    return np.ascontiguousarray(out64)


def build_qkvprep():
    p = Prog()
    ze = p.din("ze", (RPC + 1, D))
    wq = p.din("wq", (D, 1536))
    qs = p.din("qs", (1, 64))
    ks = p.din("ks", (1, 64))
    tcq = p.din("tcq", (RPC, 64))
    tsq = p.din("tsq", (RPC, 64))
    tck = p.din("tck", (RPC, 64))
    tsk = p.din("tsk", (RPC, 64))
    ident = p.din("ident", (128, 128))
    ones_r = p.din("ones_r", (1, 128))
    QR = p.dout("QR", (RPC, 512))
    KR = p.dout("KR", (RPC, 512))
    VO = p.dout("VO", (RPC, 512))

    id_sb = p.sb("idsb", 128, 128)
    on_r = p.sb("on_r", 1, 128)
    w_sb = p.sb("wsb", 128, 8 * 1536)
    qsrow = p.sb("qsrow", 1, 64)
    ksrow = p.sb("ksrow", 1, 64)
    qsb_b = p.sb("qsb_b", 128, 64)
    ksb_b = p.sb("ksb_b", 128, 64)
    hs = p.sb("hs", 128, D)
    hT = p.sb("hT", 128, D)
    qv = p.sb("qv", 128, 512)
    kv = p.sb("kv", 128, 512)
    ct = p.sb("ct", 128, 64)
    st_ = p.sb("st_", 128, 64)
    scr = p.sb("scr", 128, 64)
    nrm = p.sb("nrm", 128, 1)
    rn = p.sb("rn", 128, 1)
    qn = p.sb("qn", 128, 64)
    tmp = p.sb("tmp", 128, 64)
    t1 = p.sb("t1", 128, 64)
    orot = p.sb("orot", 128, 512)
    pT = p.ps("pT", 128, 128)
    pM = p.ps("pM", 128, 512)

    p.dma(id_sb[:, :], ident)
    p.dma(on_r[:, :], ones_r)
    p.dma(qsrow[:, :], qs)
    p.dma(ksrow[:, :], ks)
    p.bcast(qsb_b, qsrow, 64, on_r, pM)
    p.bcast(ksb_b, ksrow, 64, on_r, pM)
    p.beg()
    for kt in range(8):
        p.dma(w_sb[:, kt * 1536:(kt + 1) * 1536], wq[kt * 128:(kt + 1) * 128, :])
    p.end()

    for t in range(16):
        p.beg()
        p.dma(hs[:, 0:512], ze[1 + t * 128: 1 + (t + 1) * 128, 0:512])
        p.dma(hs[:, 512:D], ze[t * 128: t * 128 + 128, 512:D])
        p.end()
        p.transp(hT, hs, D, id_sb, pT)
        for ch, dst in ((0, qv), (1, kv)):
            p.beg()
            for kt in range(8):
                p.mm(pM[:, :], hT[:, kt * 128:(kt + 1) * 128],
                     w_sb[:, kt * 1536 + ch * 512: kt * 1536 + ch * 512 + 512],
                     kt == 0, kt == 7)
            p.end()
            p.scopy(dst[:, :], pM[:, :])
        p.beg()
        for kt in range(8):
            p.mm(pM[:, :], hT[:, kt * 128:(kt + 1) * 128],
                 w_sb[:, kt * 1536 + 1024: kt * 1536 + 1536], kt == 0, kt == 7)
        p.end()
        p.scopy(hs[:, 0:512], pM[:, :])
        p.dma(VO[t * 128:(t + 1) * 128, :], hs[:, 0:512])
        p.beg()
        for (tab, dst) in ((tcq, ct), (tsq, st_)):
            p.dma(dst[:, :], tab[t * 128:(t + 1) * 128, :])
        p.end()
        for hh in range(8):
            src = qv[:, hh * 64:(hh + 1) * 64]
            p.act(scr[:, :], src, AF.Square)
            p.vred(nrm[:, 0:1], scr[:, :], ALU.add)
            p.act(rn[:, 0:1], nrm[:, 0:1], AF.Sqrt)
            p.vts(nrm[:, 0:1], rn[:, 0:1], 1e-12, ALU.max)
            p.vrecip(rn[:, 0:1], nrm[:, 0:1])
            p.vts(qn[:, :], src, rn[:, 0:1], ALU.mult)
            p.vtt(qn[:, :], qn[:, :], qsb_b[:, :], ALU.mult)
            p.scopy(tmp[:, 0:32], qn[:, 32:64])
            p.scopy(tmp[:, 32:64], qn[:, 0:32])
            p.vtt(t1[:, :], qn[:, :], ct[:, :], ALU.mult)
            p.vtt(tmp[:, :], tmp[:, :], st_[:, :], ALU.mult)
            p.vtt(orot[:, hh * 64:(hh + 1) * 64], t1[:, :], tmp[:, :], ALU.add)
        p.dma(QR[t * 128:(t + 1) * 128, :], orot[:, :])
        p.beg()
        for (tab, dst) in ((tck, ct), (tsk, st_)):
            p.dma(dst[:, :], tab[t * 128:(t + 1) * 128, :])
        p.end()
        for hh in range(8):
            src = kv[:, hh * 64:(hh + 1) * 64]
            p.act(scr[:, :], src, AF.Square)
            p.vred(nrm[:, 0:1], scr[:, :], ALU.add)
            p.act(rn[:, 0:1], nrm[:, 0:1], AF.Sqrt)
            p.vts(nrm[:, 0:1], rn[:, 0:1], 1e-12, ALU.max)
            p.vrecip(rn[:, 0:1], nrm[:, 0:1])
            p.vts(qn[:, :], src, rn[:, 0:1], ALU.mult)
            p.vtt(qn[:, :], qn[:, :], ksb_b[:, :], ALU.mult)
            p.scopy(tmp[:, 0:32], qn[:, 32:64])
            p.scopy(tmp[:, 32:64], qn[:, 0:32])
            p.vtt(t1[:, :], qn[:, :], ct[:, :], ALU.mult)
            p.vtt(tmp[:, :], tmp[:, :], st_[:, :], ALU.mult)
            p.vtt(orot[:, hh * 64:(hh + 1) * 64], t1[:, :], tmp[:, :], ALU.add)
        p.dma(KR[t * 128:(t + 1) * 128, :], orot[:, :])
    return p.build()


def _rotary_tables():
    half = np.arange(0, 64, 2, dtype=np.float64)
    inv_freq = 1.0 / 10000.0 ** (half / 64.0)
    tt = np.arange(T, dtype=np.float64)
    freqs = tt[:, None] * inv_freq[None, :]
    freqs = np.concatenate([freqs, freqs], -1)
    base = (half + 0.4 * 64) / (1.4 * 64)
    power = (tt - T // 2) / 256.0
    scale = base[None, :] ** power[:, None]
    scale = np.concatenate([scale, scale], -1)
    cos, sin = np.cos(freqs), np.sin(freqs)
    sgn = np.concatenate([-np.ones(32), np.ones(32)])[None, :]
    tcq = cos * scale * 8.0
    tsq = sin * scale * sgn * 8.0
    tck = cos / scale
    tsk = sin / scale * sgn
    return f32(tcq), f32(tsq), f32(tck), f32(tsk)


def launch_qkvprep(z, wqkv_l, qs_l, ks_l):
    key = ("qkvprep",)
    tcq, tsq, tck, tsk = _rotary_tables()
    ident = f32(np.eye(128))
    ones_r = f32(np.ones((1, 128)))
    in_maps = []
    for c in range(NC):
        r0 = c * RPC
        prev = np.zeros((1, D), np.float32) if r0 % T == 0 else z[r0 - 1:r0]
        ze = np.concatenate([prev, z[r0:r0 + RPC]], 0)
        t0 = r0 % T
        in_maps.append({"ze": f32(ze), "wq": f32(wqkv_l),
                        "qs": f32(qs_l.reshape(1, 64)), "ks": f32(ks_l.reshape(1, 64)),
                        "tcq": tcq[t0:t0 + RPC], "tsq": tsq[t0:t0 + RPC],
                        "tck": tck[t0:t0 + RPC], "tsk": tsk[t0:t0 + RPC],
                        "ident": ident, "ones_r": ones_r})
    res = run(key, build_qkvprep, in_maps)
    qr = np.concatenate([res[c]["QR"] for c in range(NC)], 0)
    kr = np.concatenate([res[c]["KR"] for c in range(NC)], 0)
    vv = np.concatenate([res[c]["VO"] for c in range(NC)], 0)
    return qr, kr, vv


def build_locattn():
    p = Prog()
    QT = p.din("QT", (32, 64, 512))
    KT = p.din("KT", (32, 64, 1024))
    VW = p.din("VW", (32, 1024, 64))
    LM = p.din("LM", (2, 4, 128, 1024))
    ident = p.din("ident", (128, 128))
    O = p.dout("O", (32, 4, 128, 64))

    id_sb = p.sb("idsb", 128, 128)
    lm_sb = p.sb("lmsb", 128, 8 * 1024)
    qt_sb = p.sb("qtsb", 64, 512)
    kt_sb = p.sb("ktsb", 64, 1024)
    vw_sb = p.sb("vwsb", 128, 512)
    sm = p.sb("sm", 128, 1024)
    e_sb = p.sb("e_sb", 128, 1024)
    et_sb = p.sb("et_sb", 128, 1024)
    mc = p.sb("mc", 128, 1)
    nm = p.sb("nm", 128, 1)
    dc = p.sb("dc", 128, 1)
    idv = p.sb("idv", 128, 1)
    o_sb = p.sb("o_sb", 128, 64)
    pS0 = p.ps("pS0", 128, 512)
    pS1 = p.ps("pS1", 128, 512)
    pT = p.ps("pT", 128, 128)
    pO = p.ps("pO", 128, 64)

    p.beg()
    p.dma(id_sb[:, :], ident)
    for v in range(2):
        for k in range(4):
            p.dma(lm_sb[:, (v * 4 + k) * 1024:(v * 4 + k + 1) * 1024], LM[v, k])
    p.end()

    for blk in range(32):
        w0 = 1 if blk % 8 == 0 else 0
        p.beg()
        p.dma(qt_sb[:, :], QT[blk])
        p.dma(kt_sb[:, :], KT[blk])
        for jt in range(8):
            p.dma(vw_sb[:, jt * 64:(jt + 1) * 64], VW[blk, jt * 128:(jt + 1) * 128, :])
        p.end()
        for ib in range(4):
            p.beg()
            p.mm(pS0[:, :], qt_sb[:, ib * 128:(ib + 1) * 128], kt_sb[:, 0:512], True, True)
            p.mm(pS1[:, :], qt_sb[:, ib * 128:(ib + 1) * 128], kt_sb[:, 512:1024], True, True)
            p.end()
            mb = (w0 * 4 + ib) * 1024
            p.vtt(sm[:, 0:512], pS0[:, :], lm_sb[:, mb:mb + 512], ALU.add)
            p.vtt(sm[:, 512:1024], pS1[:, :], lm_sb[:, mb + 512:mb + 1024], ALU.add)
            p.vred(mc[:, 0:1], sm[:, :], ALU.max)
            p.smul(nm[:, 0:1], mc[:, 0:1], -1.0)
            p.act(e_sb[:, :], sm[:, :], AF.Exp, bias=nm[:, 0:1])
            p.vred(dc[:, 0:1], e_sb[:, :], ALU.add)
            for jt in range(8):
                p.tr(pT[:, :], e_sb[:, jt * 128:(jt + 1) * 128], id_sb)
                p.scopy(et_sb[:, jt * 128:(jt + 1) * 128], pT[:, :])
            p.beg()
            for jt in range(8):
                p.mm(pO[:, :], et_sb[:, jt * 128:(jt + 1) * 128],
                     vw_sb[:, jt * 64:(jt + 1) * 64], jt == 0, jt == 7)
            p.end()
            p.vrecip(idv[:, 0:1], dc[:, 0:1])
            p.vts(o_sb[:, :], pO[:, :], idv[:, 0:1], ALU.mult)
            p.dma(O[blk, ib], o_sb[:, :])
    return p.build()


def launch_locattn(qr, kr, vv):
    key = ("locattn",)
    q5 = qr.reshape(B, 8, 512, 8, 64)
    k5 = kr.reshape(B, 8, 512, 8, 64)
    v5 = vv.reshape(B, 8, 512, 8, 64)
    QTb = np.ascontiguousarray(q5.transpose(0, 3, 1, 4, 2)).reshape(256, 64, 512)
    K5 = np.ascontiguousarray(k5.transpose(0, 3, 1, 4, 2))
    prev_idx = [0] + list(range(7))
    KTb = np.concatenate([K5[:, :, prev_idx], K5], axis=4).reshape(256, 64, 1024)
    V5 = np.ascontiguousarray(v5.transpose(0, 3, 1, 2, 4))
    VWb = np.concatenate([V5[:, :, prev_idx], V5], axis=3).reshape(256, 1024, 64)
    il = np.arange(128)[:, None]
    lm = np.zeros((2, 4, 128, 1024), np.float32)
    for k in range(4):
        ilg = il + 128 * k
        jl = np.arange(512)[None, :]
        mprev = (jl < ilg)
        mcur = (jl > ilg)
        lm[0, k, :, 0:512] = np.where(mprev, -1e30, 0.0)
        lm[0, k, :, 512:1024] = np.where(mcur, -1e30, 0.0)
        lm[1, k, :, 0:512] = -1e30
        lm[1, k, :, 512:1024] = np.where(mcur, -1e30, 0.0)
    ident = f32(np.eye(128))
    in_maps = []
    for c in range(NC):
        sl = slice(c * 32, (c + 1) * 32)
        in_maps.append({"QT": f32(QTb[sl]), "KT": f32(KTb[sl]), "VW": f32(VWb[sl]),
                        "LM": lm, "ident": ident})
    res = run(key, build_locattn, in_maps)
    ob = np.concatenate([res[c]["O"] for c in range(NC)], 0)
    o_nat = ob.reshape(B, 8, 8, 512, 64).transpose(0, 2, 3, 1, 4).reshape(ROWS, 512)
    return np.ascontiguousarray(o_nat)


NT17 = 17


def build_innerffn(final):
    p = Prog()
    oe = p.din("oe", (RPC + 128, 512))
    ze = p.din("ze", (RPC + 128, D))
    wo = p.din("wo", (512, D))
    lg = p.din("lg", (1, D))
    lb = p.din("lb", (1, D))
    w1 = p.din("w1", (D, 5460))
    b1 = p.din("b1", (5460, 1))
    w2 = p.din("w2", (2730, D))
    b2 = p.din("b2", (1, D))
    zmask = p.din("zmask", (1, 512))
    ident = p.din("ident", (128, 128))
    ones_r = p.din("ones_r", (1, 128))
    if final:
        x3 = p.din("x3", (RPC, D))
        out = p.dout("out", (RPC, D))
    else:
        out = p.dout("out", (RPC, D))
    z1s = p.nc.dram_tensor("z1s", [RPC + 128, D], F32, kind="Internal").ap()

    id_sb = p.sb("idsb", 128, 128)
    on_r = p.sb("on_r", 1, 128)
    wo_sb = p.sb("wosb", 128, 4 * D)
    lgb = p.sb("lgb", 128, D)
    lbb = p.sb("lbb", 128, D)
    b2b = p.sb("b2b", 128, D)
    zm = p.sb("zm", 1, 512)
    row = p.sb("row", 1, D)
    ot = p.sb("ot", 128, 512)
    oT = p.sb("oT", 128, 512)
    zt = p.sb("zt", 128, D)
    hs = p.sb("hs", 128, D)
    hn = p.sb("hn", 128, D)
    scr = p.sb("scr", 128, D)
    c1 = p.sb("c1", 128, 1)
    c2 = p.sb("c2", 128, 1)
    hnT = p.sb("hnT", 128, 8 * RPC)
    y3 = p.sb("y3", 128, 16 * D)
    w1a = p.sb("w1a", 128, 8 * 128)
    w1g = p.sb("w1g", 128, 8 * 128)
    b1a = p.sb("b1a", 128, 1)
    b1g = p.sb("b1g", 128, 1)
    w2c = p.sb("w2c", 128, D)
    glu = p.sb("glu", 128, 512)
    gel = p.sb("gel", 128, 512)
    pT = p.ps("pT", 128, 128)
    pM = p.ps("pM", 128, 512)
    pA = p.ps("pA", 128, 512)
    pG = p.ps("pG", 128, 512)

    p.beg()
    p.dma(id_sb[:, :], ident)
    p.dma(on_r[:, :], ones_r)
    p.dma(zm[:, :], zmask)
    for kt in range(4):
        p.dma(wo_sb[:, kt * D:(kt + 1) * D], wo[kt * 128:(kt + 1) * 128, :])
    p.end()
    for (r, dst) in ((lg, lgb), (lb, lbb), (b2, b2b)):
        p.dma(row[:, :], r)
        p.bcast(dst, row, D, on_r, pM)

    for t in range(NT17):
        p.beg()
        p.dma(ot[:, :], oe[t * 128:(t + 1) * 128, :])
        p.dma(zt[:, :], ze[t * 128:(t + 1) * 128, :])
        p.end()
        p.transp(oT, ot, 512, id_sb, pT)
        for o in range(0, D, 512):
            p.beg()
            for kt in range(4):
                p.mm(pM[:, :], oT[:, kt * 128:(kt + 1) * 128],
                     wo_sb[:, kt * D + o: kt * D + o + 512], kt == 0, kt == 3)
            p.end()
            p.vtt(zt[:, o:o + 512], zt[:, o:o + 512], pM[:, :], ALU.add)
        p.dma(z1s[t * 128:(t + 1) * 128, :], zt[:, :])
    for t in range(16):
        r0 = 128 + t * 128
        p.beg()
        p.dma(hs[:, 0:512], z1s[r0:r0 + 128, 0:512])
        p.dma(hs[:, 512:D], z1s[r0 - 1:r0 + 127, 512:D])
        p.end()
        if t == 0:
            p.vtt(hs[0:1, 512:D], hs[0:1, 512:D], zm[0:1, :], ALU.mult)
        p.ln(hn[:, :], hs[:, :], D, scr[:, :], c1, c2)
        p.vtt(hn[:, :], hn[:, :], lgb[:, :], ALU.mult)
        p.vtt(hn[:, :], hn[:, :], lbb[:, :], ALU.add)
        for kt in range(8):
            p.tr(pT[:, :], hn[:, kt * 128:(kt + 1) * 128], id_sb)
            p.scopy(hnT[:, kt * RPC + t * 128: kt * RPC + (t + 1) * 128], pT[:, :])
    for t in range(16):
        p.dma(zt[:, :], z1s[128 + t * 128: 128 + (t + 1) * 128, :])
        p.vtt(y3[:, t * D:(t + 1) * D], zt[:, :], b2b[:, :], ALU.add)
    chunks = []
    c0 = 0
    while c0 < 2730:
        sk = min(128, 2730 - c0)
        chunks.append((c0, sk))
        c0 += sk
    for (cc, sk) in chunks:
        p.beg()
        p.dma(w1a[:, 0:8 * sk].rearrange("c (ct n) -> c ct n", ct=8),
              w1[:, cc:cc + sk].rearrange("(ct c) n -> c ct n", c=128))
        p.dma(w1g[:, 0:8 * sk].rearrange("c (ct n) -> c ct n", ct=8),
              w1[:, 2730 + cc:2730 + cc + sk].rearrange("(ct c) n -> c ct n", c=128))
        p.dma(b1a[0:sk, :], b1[cc:cc + sk, :])
        p.dma(b1g[0:sk, :], b1[2730 + cc:2730 + cc + sk, :])
        p.dma(w2c[0:sk, :], w2[cc:cc + sk, :])
        p.end()
        for th in range(4):
            tb0 = th * 512
            p.beg()
            for kt in range(8):
                p.mm(pA[0:sk, :], w1a[:, kt * sk:(kt + 1) * sk],
                     hnT[:, kt * RPC + tb0: kt * RPC + tb0 + 512], kt == 0, kt == 7)
                p.mm(pG[0:sk, :], w1g[:, kt * sk:(kt + 1) * sk],
                     hnT[:, kt * RPC + tb0: kt * RPC + tb0 + 512], kt == 0, kt == 7)
            p.end()
            p.act(gel[0:sk, :], pG[0:sk, :], AF.Gelu, bias=b1g[0:sk, 0:1])
            p.vstt(glu[0:sk, :], pA[0:sk, :], b1a[0:sk, 0:1], gel[0:sk, :],
                   ALU.add, ALU.mult)
            for tb in range(4):
                tt = th * 4 + tb
                for o in range(0, D, 512):
                    p.mm(pM[:, :], glu[0:sk, tb * 128:(tb + 1) * 128],
                         w2c[0:sk, o:o + 512], True, True)
                    p.vtt(y3[:, tt * D + o: tt * D + o + 512],
                          y3[:, tt * D + o: tt * D + o + 512], pM[:, :], ALU.add)
    for t in range(16):
        if final:
            p.ln(hn[:, :], y3[:, t * D:(t + 1) * D], D, scr[:, :], c1, c2)
            p.dma(zt[:, :], x3[t * 128:(t + 1) * 128, :])
            p.vtt(hn[:, :], hn[:, :], zt[:, :], ALU.add)
            p.dma(out[t * 128:(t + 1) * 128, :], hn[:, :])
        else:
            p.scopy(hn[:, :], y3[:, t * D:(t + 1) * D])
            p.dma(out[t * 128:(t + 1) * 128, :], hn[:, :])
    return p.build()


def launch_innerffn(z, o_nat, wo_l, lg_l, lb_l, w1_l, b1_l, w2_l, b2_l,
                    final=False, x3=None):
    key = ("innerffn", final)
    ident = f32(np.eye(128))
    ones_r = f32(np.ones((1, 128)))
    in_maps = []
    for c in range(NC):
        r0 = c * RPC
        if r0 == 0:
            pz = np.zeros((128, D), np.float32)
            po = np.zeros((128, 512), np.float32)
        else:
            pz = z[r0 - 128:r0]
            po = o_nat[r0 - 128:r0]
        zmask = np.zeros((1, 512), np.float32) if r0 % T == 0 else np.ones((1, 512), np.float32)
        m = {"oe": f32(np.concatenate([po, o_nat[r0:r0 + RPC]], 0)),
             "ze": f32(np.concatenate([pz, z[r0:r0 + RPC]], 0)),
             "wo": f32(wo_l), "lg": f32(lg_l.reshape(1, D)), "lb": f32(lb_l.reshape(1, D)),
             "w1": f32(w1_l), "b1": f32(b1_l.reshape(5460, 1)), "w2": f32(w2_l),
             "b2": f32(b2_l.reshape(1, D)), "zmask": zmask,
             "ident": ident, "ones_r": ones_r}
        if final:
            m["x3"] = f32(x3[r0:r0 + RPC])
        in_maps.append(m)
    res = run(key, lambda: build_innerffn(final), in_maps)
    return np.concatenate([res[c]["out"] for c in range(NC)], 0)


def kernel(L0_attn_x, L0_attn_Wq, L0_attn_Wk, L0_attn_Wv, L0_attn_Wout, L0_attn_betas,
           L0_ffn_W_r, L0_ffn_W_k, L0_ffn_W_v, L0_ffn_mu_r, L0_ffn_mu_k,
           L1_attn_w_qkv, L1_attn_w_out, L1_ffn_wqkv, L1_ffn_q_scale, L1_ffn_k_scale,
           L1_ffn_wo, L1_ffn_ln_g, L1_ffn_ln_b, L1_ffn_w1, L1_ffn_b1, L1_ffn_w2,
           L1_ffn_b2):
    x0 = f32(L0_attn_x).reshape(ROWS, D)
    qf, kf, vf = launch_lnproj(x0, [f32(L0_attn_Wq).T, f32(L0_attn_Wk).T,
                                    f32(L0_attn_Wv).T])
    att = launch_l0attn(qf, kf, vf, f32(L0_attn_betas))
    x1 = launch_projres(att, f32(L0_attn_Wout).T, x0, out_z=False)
    x2 = launch_l0ffn(x1, f32(L0_ffn_W_r).T, f32(L0_ffn_W_k).T, f32(L0_ffn_W_v).T,
                      f32(L0_ffn_mu_r), f32(L0_ffn_mu_k))
    (qkv,) = launch_lnproj(x2, [f32(L1_attn_w_qkv).T])
    out64 = launch_l1attn(qkv)
    x3, z = launch_projres(out64, f32(L1_attn_w_out).T, x2, out_z=True)
    cur = z
    for l in range(2):
        qr, kr, vv = launch_qkvprep(cur, f32(L1_ffn_wqkv)[l], f32(L1_ffn_q_scale)[l],
                                    f32(L1_ffn_k_scale)[l])
        o_nat = launch_locattn(qr, kr, vv)
        cur = launch_innerffn(cur, o_nat, f32(L1_ffn_wo)[l], f32(L1_ffn_ln_g)[l],
                              f32(L1_ffn_ln_b)[l], f32(L1_ffn_w1)[l], f32(L1_ffn_b1)[l],
                              f32(L1_ffn_w2)[l], f32(L1_ffn_b2)[l],
                              final=(l == 1), x3=x3)
    return cur.reshape(B, T, D).astype(np.float32)
